# version 20
# baseline (speedup 1.0000x reference)
import numpy as np
from contextlib import ExitStack

import concourse.bass as bass
import concourse.mybir as mybir
from concourse.bass_utils import run_bass_kernel_spmd
from concourse.bass_core_rust import IndirectOffsetOnAxis
from concourse.library_config import local_scatter as LS_LIB, standard as STD_LIB

F32 = mybir.dt.float32
BF16 = mybir.dt.bfloat16
AF = mybir.ActivationFunctionType
ALU = mybir.AluOpType
AX = mybir.AxisListType
I16 = mybir.dt.int16
U32 = mybir.dt.uint32

T = 4096
D = 1024
NT = T // 128
NPAIR = 8
NE = 8
HID = 4096
NHC = HID // 128
NQC = T // 512
NKC = T // 128
NE_RUN = NE
DEBUG_TAPS = False
TB_RUN = 8
BIS_ITERS = 34

ENGS = ("sync", "gpsimd", "vector", "scalar", "tensor")


class Ctr:
    def __init__(self, nc, es, name):
        self.sem = es.enter_context(nc.semaphore(name))
        self.n = 0

    def bump(self):
        self.n += 1
        return self.n


class DmaQ:

    def __init__(self, nc, es, name, depth=6):
        self.sems = [es.enter_context(nc.semaphore(f"{name}{i}")) for i in range(depth)]
        self.cnt = [0] * depth
        self.i = 0
        self.depth = depth

    def issue(self):
        k = self.i % self.depth
        pre = (self.sems[k], self.cnt[k]) if self.cnt[k] > 0 else None
        self.cnt[k] += 16
        self.i += 1
        return self.sems[k], self.cnt[k], pre

    def total_waits(self):
        return [(s, c) for s, c in zip(self.sems, self.cnt) if c > 0]


def build():
    nc = bass.Bass("TRN2", target_bir_lowering=False)

    x_in = nc.dram_tensor("x", [T, D], F32, kind="ExternalInput")
    ln_g = nc.dram_tensor("ln_g", [1, D], F32, kind="ExternalInput").ap()
    ln_b = nc.dram_tensor("ln_b", [1, D], F32, kind="ExternalInput").ap()
    w_qkv = nc.dram_tensor("w_qkv", [D, 3 * D], F32, kind="ExternalInput")
    w_out = nc.dram_tensor("w_out", [D, D], F32, kind="ExternalInput")
    b_out = nc.dram_tensor("b_out", [1, D], F32, kind="ExternalInput").ap()
    gate_w = nc.dram_tensor("gate_w", [D, NE], F32, kind="ExternalInput")
    w1 = nc.dram_tensor("w1", [NE, D, HID], F32, kind="ExternalInput")
    w2 = nc.dram_tensor("w2", [NE, HID, D], F32, kind="ExternalInput")
    out = nc.dram_tensor("out", [T, D], F32, kind="ExternalOutput")

    x1_scr = nc.dram_tensor("x1_scr", [T, D], BF16, kind="Internal")
    if DEBUG_TAPS:
        dbg_h = nc.dram_tensor("dbg_h", [128, NHC, 512], BF16, kind="Internal")
        dbg_y = nc.dram_tensor("dbg_y", [128, 8, 512], BF16, kind="Internal")
    meta_scr = nc.dram_tensor("meta_scr", [T, NE], BF16, kind="Internal")
    toksl_scr = nc.dram_tensor("toksl_scr", [NE * 1152, 1], U32, kind="Internal")
    r3_scr = nc.dram_tensor("r3_scr", [T, D], F32, kind="Internal")
    w1bf = nc.dram_tensor("w1bf", [NE, D, HID], BF16, kind="Internal")
    w2bf = nc.dram_tensor("w2bf", [NE, HID, D], BF16, kind="Internal")

    plan = {e: [] for e in ENGS}

    def on(e, fn):
        plan[e].append(fn)

    def wt(e, sem, val):
        if isinstance(sem, Ctr):
            sem = sem.sem
        plan[e].append(lambda eng, s=sem, v=val: eng.wait_ge(s, v))

    es = ExitStack()
    with es:
        pe = Ctr(nc, es, "pe")
        ve = Ctr(nc, es, "ve")
        ac = Ctr(nc, es, "ac")
        ldx = DmaQ(nc, es, "ldx", 6)
        stx = DmaQ(nc, es, "stx", 6)
        ldw = DmaQ(nc, es, "ldw", 8)
        acq = DmaQ(nc, es, "acq", 4)
        mvq = DmaQ(nc, es, "mvq", 4)
        ysq = DmaQ(nc, es, "ysq", 8)
        wcq = [DmaQ(nc, es, f"wc{i}", 2) for i in range(2)]
        wfq = DmaQ(nc, es, "wfq", 8)
        wfq2 = DmaQ(nc, es, "wfq2", 4)

        def sb(name, shape, dt, stack=es):
            return stack.enter_context(nc.sbuf_tensor(name, shape, dt))

        c_eps = sb("c_eps", [128, 1], F32)
        c_m20 = sb("c_m20", [128, 1], F32)
        ones32 = sb("ones32", [1, 128], F32)
        ones8 = sb("ones8", [8, 128], BF16)
        id16 = sb("id16", [128, 128], BF16)
        id32 = sb("id32", [128, 128], F32)
        selrep = sb("selrep", [8, 128], F32)
        zt = sb("zt", [128, 512], F32)
        gate_wb = sb("gate_wb", [128, 8, NE], BF16)
        bnr = sb("bnr", [128, 2, 2, 6], F32)
        mvr = sb("mvr", [128, 2, 2], F32)
        rsr = sb("rsr", [128, 2, 1], F32)
        bigB = sb("bigB", [128, 8, T], BF16)

        ps = es.enter_context(nc.psum_tensor("ps", [128, 4096], F32))

        def bank(i, n=1):
            return ps[:, i * 512:(i + n) * 512]

        def bank16(i):
            return ps[:, i * 512:(i + 1) * 512].bitcast(BF16)

        bank_guard = {i: None for i in range(8)}

        def wait_bank(i, eng="tensor"):
            g = bank_guard[i]
            if g is not None:
                wt(eng, g[0], g[1])
                bank_guard[i] = None

        def set_guard(i, ctr, val):
            bank_guard[i] = (ctr.sem, val)

        def mkwaits(waits):
            res = []
            for s, v in waits:
                res.append((s.sem if isinstance(s, Ctr) else s, v))
            return tuple(res)

        def dma(q_eng, stream, out_ap, in_ap, waits=()):
            semh, tgt, pre = stream.issue()
            w = mkwaits(waits)
            def f(eng, o=out_ap, i=in_ap, w=w, s=semh, pre=pre):
                if pre is not None:
                    eng.wait_ge(pre[0], pre[1])
                for ws, wv in w:
                    eng.wait_ge(ws, wv)
                eng.dma_start(out=o, in_=i).then_inc(s, 16)
            on(q_eng, f)
            return semh, tgt

        def dma_acc(stream, out_ap, in_ap, waits=(), accum=True):
            semh, tgt, pre = stream.issue()
            w = mkwaits(waits)
            aop = ALU.add if accum else ALU.bypass
            def f(eng, o=out_ap, i=in_ap, w=w, s=semh, aop=aop, pre=pre):
                if pre is not None:
                    eng.wait_ge(pre[0], pre[1])
                for ws, wv in w:
                    eng.wait_ge(ws, wv)
                eng.dma_start(out=o, in_=i, accum_op=aop).then_inc(s, 16)
            on("gpsimd", f)
            return semh, tgt

        def op(e, fn, bump=None):
            if bump is None:
                on(e, fn)
                return None
            v = bump.bump()
            def f(eng, fn=fn, s=bump.sem):
                fn(eng).then_inc(s, 1)
            on(e, f)
            return v

        def drain(e):
            on(e, lambda eng: eng.drain())

        def phase_waits():
            w = [(pe, pe.n), (ve, ve.n), (ac, ac.n)]
            for q in (ldx, stx, ldw, acq, mvq, wfq, wfq2, *wcq):
                w += q.total_waits()
            return w

        for ap_, val in ((c_eps[:, :], 1e-5), (c_m20[:, :], -20.0),
                         (ones32[:, :], 1.0), (ones8[:, :], 1.0),
                         (id16[:, :], 1.0), (id32[:, :], 1.0)):
            on("gpsimd", lambda e, a=ap_, v=val: e.memset(a, v))
        gp0 = Ctr(nc, es, "gp0")
        op("gpsimd", lambda e: e.engine_nop(), bump=gp0)
        wt("gpsimd", gp0, gp0.n)
        on("gpsimd", lambda e: e.affine_select(
            id16[:, :], id16[:, :], [[1, 128]], ALU.is_equal, 0.0,
            base=0, channel_multiplier=-1))
        on("gpsimd", lambda e: e.affine_select(
            id32[:, :], id32[:, :], [[1, 128]], ALU.is_equal, 0.0,
            base=0, channel_multiplier=-1))
        on("gpsimd", lambda e: e.memset(zt[:, :], 0.0))
        bc_regs = [nc.gpsimd.alloc_register(f"bce{e_}") for e_ in range(NE)]
        for e_ in range(NE):
            on("gpsimd", lambda e, e_=e_: e.reg_mov(bc_regs[e_], e_ * 1152 + 1023))
        on("gpsimd", lambda e: e.memset(selrep[:, :], 1.0))
        gp1 = Ctr(nc, es, "gp1")
        op("gpsimd", lambda e: e.engine_nop(), bump=gp1)
        wt("gpsimd", gp1, gp1.n)
        on("gpsimd", lambda e: e.affine_select(
            selrep[:, :], selrep[:, :], [[0, 16], [1, 8]], ALU.is_equal, 0.0,
            base=0, channel_multiplier=-1))
        gp_consts = Ctr(nc, es, "gpc")
        op("gpsimd", lambda e: e.engine_nop(), bump=gp_consts)

        dma("gpsimd", ldw, gate_wb[:, :, :],
            bass.AP(gate_w, 0, [[NE, 128], [NE * 128, 8], [1, NE]]))
        w_gate = ldw.total_waits()
        drain("gpsimd")
        for zi in range(NT):
            for h in range(2):
                dma_acc(acq, bass.AP(r3_scr, zi * 128 * D + h * 512,
                                     [[D, 128], [1, 512]]),
                        zt[:, :], accum=False)
        r3z_waits = acq.total_waits()
        wconv = {}

        def emit_wconv(e_):
            a = dma("gpsimd", wcq[0],
                    bass.AP(w1bf, e_ * D * HID, [[HID, D], [2048, 2], [1, 2048]]),
                    bass.AP(w1, e_ * D * HID, [[HID, D], [2048, 2], [1, 2048]]))
            b = dma("gpsimd", wcq[1],
                    bass.AP(w2bf, e_ * HID * D, [[D, HID], [512, 2], [1, 512]]),
                    bass.AP(w2, e_ * HID * D, [[D, HID], [512, 2], [1, 512]]))
            wconv[e_] = [a, b]

        s1 = es.enter_context(ExitStack())
        xnT = sb("xnT", [128, 8, T], BF16, stack=s1)
        s1a = ExitStack()
        g_rep = sb("g_rep", [128, D], F32, stack=s1a)
        b_rep = sb("b_rep", [128, D], F32, stack=s1a)
        xnr = sb("xnr", [128, 2, D], F32, stack=s1a)
        xnbr = sb("xnbr", [128, 2, D], BF16, stack=s1a)
        xr1 = sb("xr", [128, 3, D], F32, stack=s1a)
        gbuf = sb("gbuf", [1, D], F32, stack=s1a)
        bbuf = sb("bbuf", [1, D], F32, stack=s1a)
        dma("sync", ldx, gbuf[:, :], ln_g)
        dma("sync", ldx, bbuf[:, :], ln_b)
        ld_params = ldx.total_waits()

        wt("tensor", gp_consts, gp_consts.n)
        def grep_mms(eng):
            for s, v in ld_params:
                eng.wait_ge(s, v)
            eng.matmul(bank(0), ones32[:, :], gbuf[:, 0:512], start=True, stop=True)
            eng.matmul(bank(1), ones32[:, :], gbuf[:, 512:1024], start=True, stop=True)
            eng.matmul(bank(2), ones32[:, :], bbuf[:, 0:512], start=True, stop=True)
            return eng.matmul(bank(3), ones32[:, :], bbuf[:, 512:1024], start=True, stop=True)
        v_grep = op("tensor", grep_mms, bump=pe)
        wt("vector", pe, v_grep)
        op("vector", lambda e: e.tensor_copy(g_rep[:, :], bank(0, 2)))
        v_reps = op("vector", lambda e: e.tensor_copy(b_rep[:, :], bank(2, 2)), bump=ve)
        for i in range(4):
            set_guard(i, ve, v_reps)

        xr1_free = [None, None, None]
        for t in range(NT):
            sl = t % 2
            slx = t % 3
            w = [] if xr1_free[slx] is None else [(ve, xr1_free[slx])]
            s_ld, v_ld = dma("sync", ldx, xr1[:, slx, :],
                             bass.AP(x_in, t * 128 * D, [[D, 128], [1, D]]),
                             waits=w)
            wt("vector", s_ld, v_ld)
            op("vector", lambda e, sl=sl, slx=slx: e.bn_stats(bnr[:, sl, 0, :], xr1[:, slx, 0:512]))
            op("vector", lambda e, sl=sl, slx=slx: e.bn_stats(bnr[:, sl, 1, :], xr1[:, slx, 512:1024]))
            drain("vector")
            v_bn = op("vector", lambda e, sl=sl: e.bn_aggr(mvr[:, sl, :], bnr[:, sl, :, :]), bump=ve)
            wt("scalar", ve, v_bn)
            op("scalar", lambda e, sl=sl: e.activation(rsr[:, sl, :], mvr[:, sl, 1:2], AF.Ln, bias=c_eps[:, :]))
            drain("scalar")
            v_rs = op("scalar", lambda e, sl=sl: e.activation(rsr[:, sl, :], rsr[:, sl, :], AF.Exp, scale=-0.5), bump=ac)
            wt("vector", ac, v_rs)
            op("vector", lambda e, sl=sl, slx=slx: e.tensor_scalar(
                xnr[:, sl, :], xr1[:, slx, :], mvr[:, sl, 0:1], rsr[:, sl, 0:1],
                ALU.subtract, ALU.mult))
            drain("vector")
            op("vector", lambda e, sl=sl: e.tensor_mul(xnr[:, sl, :], xnr[:, sl, :], g_rep[:, :]))
            drain("vector")
            v_xnb = op("vector", lambda e, sl=sl: e.tensor_tensor(
                xnbr[:, sl, :], xnr[:, sl, :], b_rep[:, :], ALU.add), bump=ve)
            xr1_free[slx] = v_xnb
            tb = 4 + sl
            wait_bank(tb)
            wt("tensor", ve, v_xnb)
            def tr_all(eng, sl=sl, tb=tb):
                for c in range(8):
                    ins = eng.matmul(bank16(tb)[:, c * 128:(c + 1) * 128],
                                     xnbr[:, sl, c * 128:(c + 1) * 128],
                                     id16[:, :], start=True, stop=True,
                                     is_transpose=True)
                return ins
            v_tr = op("tensor", tr_all, bump=pe)
            wt("vector", pe, v_tr)
            v_ev = op("vector", lambda e, t=t, tb=tb: e.tensor_copy(
                xnT[:, :, t * 128:(t + 1) * 128],
                bank16(tb).rearrange("p (c j) -> p c j", c=8)), bump=ve)
            set_guard(tb, ve, v_ev)
        s1a.close()

        s1b = es.enter_context(ExitStack())
        wqs = sb("wqs", [128, 8, 128], BF16, stack=s1b)
        wks = sb("wks", [128, 8, 128], BF16, stack=s1b)
        wvs = sb("wvs", [128, 8, 128], BF16, stack=s1b)
        qT = sb("qT", [128, T], BF16, stack=s1b)
        kT = sb("kT", [128, T], BF16, stack=s1b)
        vauga = sb("vauga", [128, NKC, 128], BF16, stack=s1b)
        vaugb = sb("vaugb", [128, NKC, 128], BF16, stack=s1b)
        pt = sb("pt", [128, 2, 3, 512], BF16, stack=s1b)
        rz_src = sb("rz_src", [128, 2, 512], F32, stack=s1b)
        rz_dst = sb("rz_dst", [128, 2, 512], F32, stack=s1b)
        onorm = sb("onorm", [128, 2, 2, 512], F32, stack=s1b)

        def w_slice_ap(col0):
            return bass.AP(w_qkv, col0, [[3 * D, 128], [3 * D * 128, 8], [1, 128]])

        wpair_tgt = [None] * NPAIR

        def load_pair_w(p):
            w = [(pe, pe.n)]
            a = dma("gpsimd", ldw, wqs[:, :, :], w_slice_ap(p * 128), waits=w)
            b = dma("gpsimd", ldw, wks[:, :, :], w_slice_ap(D + p * 128))
            c = dma("gpsimd", ldw, wvs[:, :, :], w_slice_ap(2 * D + p * 128))
            wpair_tgt[p] = [a, b, c]
            emit_wconv(p)

        load_pair_w(0)

        rz_src_free = [None, None]
        rz_dst_free = [None, None]
        onorm_free = [None, None]
        pt_free = [None, None, None]

        for p in range(NPAIR):
            sl = p % 2
            for s, v in wpair_tgt[p]:
                wt("tensor", s, v)
            wt("vector", pe, pe.n)
            op("vector", lambda e: e.memset(vauga[:, :, 64:128], 1.0))
            v_vones = op("vector", lambda e: e.memset(vaugb[:, :, 0:64], 1.0), bump=ve)
            wt("tensor", ve, v_vones)

            last_evac = None
            for qc in range(NQC):
                b_ = qc % 2
                wait_bank(b_)
                def qmm(eng, sl=sl, qc=qc, b_=b_):
                    for c in range(8):
                        ins = eng.matmul(bank(b_), wqs[:, c, :],
                                         xnT[:, c, qc * 512:(qc + 1) * 512],
                                         start=(c == 0), stop=(c == 7))
                    return ins
                v_mm = op("tensor", qmm, bump=pe)
                wt("vector", pe, v_mm)
                v_e = op("vector", lambda e, qc=qc, b_=b_: e.tensor_copy(
                    qT[:, qc * 512:(qc + 1) * 512], bank(b_)), bump=ve)
                set_guard(b_, ve, v_e)
            for qc in range(NQC):
                b_ = 2 + qc % 2
                wait_bank(b_)
                def kmm(eng, sl=sl, qc=qc, b_=b_):
                    for c in range(8):
                        ins = eng.matmul(bank(b_), wks[:, c, :],
                                         xnT[:, c, qc * 512:(qc + 1) * 512],
                                         start=(c == 0), stop=(c == 7))
                    return ins
                v_mm = op("tensor", kmm, bump=pe)
                wt("vector", pe, v_mm)
                v_e = op("vector", lambda e, qc=qc, b_=b_: e.tensor_copy(
                    kT[:, qc * 512:(qc + 1) * 512], bank(b_)), bump=ve)
                set_guard(b_, ve, v_e)
            for g in range(8):
                b_ = 4 + g % 2
                wait_bank(b_)
                def vmm(eng, sl=sl, g=g, b_=b_):
                    for ttl in range(4):
                        tt = g * 4 + ttl
                        for c in range(8):
                            ins = eng.matmul(
                                bank(b_)[:, ttl * 128:(ttl + 1) * 128],
                                xnT[:, c, tt * 128:(tt + 1) * 128],
                                wvs[:, c, :],
                                start=(c == 0), stop=(c == 7))
                    return ins
                v_mm = op("tensor", vmm, bump=pe)
                wt("vector", pe, v_mm)
                op("vector", lambda e, g=g, b_=b_: e.tensor_copy(
                    vauga[:, g * 4:(g + 1) * 4, 0:64],
                    bank(b_).rearrange("p (t j) -> p t j", t=4)[:, :, 0:64]))
                v_e = op("vector", lambda e, g=g, b_=b_: e.tensor_copy(
                    vaugb[:, g * 4:(g + 1) * 4, 64:128],
                    bank(b_).rearrange("p (t j) -> p t j", t=4)[:, :, 64:128]), bump=ve)
                set_guard(b_, ve, v_e)
                last_evac = v_e
            wt("tensor", ve, last_evac)
            if p + 1 < NPAIR:
                load_pair_w(p + 1)

            for qc in range(NQC):
                oa = 6
                ob = 7
                wait_bank(oa)
                wait_bank(ob)
                pv_last = None
                exp_v = [None] * NKC
                pt_free3 = [None, None, None]

                def emit_s(kc, qc=qc):
                    sb0 = 2 * (kc % 3)
                    wait_bank(sb0)
                    wait_bank(sb0 + 1)
                    def smm(eng, qc=qc, kc=kc, sb0=sb0):
                        eng.matmul(bank(sb0), kT[0:64, kc * 128:(kc + 1) * 128],
                                   qT[0:64, qc * 512:(qc + 1) * 512],
                                   start=True, stop=True)
                        return eng.matmul(bank(sb0 + 1),
                                          kT[64:128, kc * 128:(kc + 1) * 128],
                                          qT[64:128, qc * 512:(qc + 1) * 512],
                                          start=True, stop=True)
                    v_s = op("tensor", smm, bump=pe)
                    pr = kc % 3
                    wt("scalar", pe, v_s)
                    if pt_free3[pr] is not None:
                        wt("scalar", pe, pt_free3[pr])
                    v_x = op("scalar", lambda e, pr=pr, sb0=sb0: e.activation(
                        pt[:, :, pr, :], bank(sb0, 2), AF.Exp,
                        bias=c_m20[:, :], scale=0.125), bump=ac)
                    set_guard(sb0, ac, v_x)
                    set_guard(sb0 + 1, ac, v_x)
                    exp_v[kc] = v_x

                def emit_pv(kc, qc=qc, oa=oa, ob=ob):
                    pr = kc % 3
                    wt("tensor", ac, exp_v[kc])
                    def pvmm(eng, kc=kc, pr=pr, oa=oa, ob=ob):
                        eng.matmul(bank(oa), vauga[:, kc, :], pt[:, 0, pr, :],
                                   start=(kc == 0), stop=(kc == NKC - 1))
                        return eng.matmul(bank(ob), vaugb[:, kc, :], pt[:, 1, pr, :],
                                          start=(kc == 0), stop=(kc == NKC - 1))
                    v_pv = op("tensor", pvmm, bump=pe)
                    pt_free3[pr] = v_pv
                    return v_pv

                emit_s(0)
                emit_s(1)
                for kc in range(2, NKC):
                    emit_s(kc)
                    pv_last = emit_pv(kc - 2)
                pv_last = emit_pv(NKC - 2)
                pv_last = emit_pv(NKC - 1)
                rs_ = qc % 2
                wt("vector", pe, pv_last)
                if onorm_free[rs_] is not None:
                    wt("vector", onorm_free[rs_][0], onorm_free[rs_][1])
                op("vector", lambda e, rs_=rs_: e.tensor_copy(
                    onorm[:, rs_, 0, :], bank(6)))
                v_oe = op("vector", lambda e, rs_=rs_: e.tensor_copy(
                    onorm[:, rs_, 1, :], bank(7)), bump=ve)
                set_guard(6, ve, v_oe)
                set_guard(7, ve, v_oe)
                wt("scalar", ve, v_oe)
                if rz_src_free[rs_] is not None:
                    wt("scalar", rz_src_free[rs_][0], rz_src_free[rs_][1])
                op("scalar", lambda e, rs_=rs_: e.activation(
                    rz_src[64:128, rs_, :], onorm[64:128, rs_, 0, :], AF.Ln))
                op("scalar", lambda e, rs_=rs_: e.activation(
                    rz_src[0:64, rs_, :], onorm[0:64, rs_, 1, :], AF.Ln))
                drain("scalar")
                op("scalar", lambda e, rs_=rs_: e.activation(
                    rz_src[64:128, rs_, :], rz_src[64:128, rs_, :], AF.Exp, scale=-1.0))
                v_rz = op("scalar", lambda e, rs_=rs_: e.activation(
                    rz_src[0:64, rs_, :], rz_src[0:64, rs_, :], AF.Exp, scale=-1.0), bump=ac)
                onorm_free[rs_] = (ac.sem, v_rz)
                mw = [(ac, v_rz)]
                if rz_dst_free[rs_] is not None:
                    mw.append(rz_dst_free[rs_])
                m1s, m1v = dma("scalar", mvq, rz_dst[0:64, rs_, :],
                               rz_src[64:128, rs_, :], waits=mw)
                m2s, m2v = dma("scalar", mvq, rz_dst[64:128, rs_, :],
                               rz_src[0:64, rs_, :])
                rz_src_free[rs_] = (m2s, m2v)
                wt("vector", m1s, m1v)
                wt("vector", m2s, m2v)
                op("vector", lambda e, p=p, qc=qc, rs_=rs_: e.tensor_mul(
                    bigB[0:64, p, qc * 512:(qc + 1) * 512],
                    onorm[0:64, rs_, 0, :], rz_dst[0:64, rs_, :]))
                v_n = op("vector", lambda e, p=p, qc=qc, rs_=rs_: e.tensor_mul(
                    bigB[64:128, p, qc * 512:(qc + 1) * 512],
                    onorm[64:128, rs_, 1, :], rz_dst[64:128, rs_, :]), bump=ve)
                rz_dst_free[rs_] = (ve, v_n)
        s1b.close()
        s1.close()

        probs_et = sb("probs_et", [8, T], F32)
        probs_tok = sb("probs_tok", [128, NT, NE], F32)
        slot_tok = sb("slot_tok", [128, NT, NE], U32)
        s2 = es.enter_context(ExitStack())
        w_outb = sb("w_outb", [128, 8, D], BF16, stack=s2)
        borep = sb("borep", [128, D], F32, stack=s2)
        bobuf = sb("bobuf", [1, D], F32, stack=s2)
        dma("sync", ldx, bobuf[:, :], b_out, waits=phase_waits())
        ld_bo = ldx.total_waits()
        r2r = sb("r2r", [128, 2, D], F32, stack=s2)
        x1br = sb("x1br", [128, 2, D], BF16, stack=s2)
        sexp = sb("sexp", [128, 2, NE], F32, stack=s2)
        xr2 = sb("xr2", [128, 3, D], F32, stack=s2)
        xr2_free = [None, None, None]
        srec = sb("srec", [128, 2, 1], F32, stack=s2)

        dma("gpsimd", ldw, w_outb[:, :, :],
            bass.AP(w_out, 0, [[D, 128], [D * 128, 8], [1, D]]),
            waits=phase_waits())
        w_wout = ldw.total_waits()

        wait_bank(0)
        wait_bank(1)
        wt("tensor", ve, ve.n)
        def borep_mms(eng):
            for s, v in ld_bo:
                eng.wait_ge(s, v)
            eng.matmul(bank(0), ones32[:, :], bobuf[:, 0:512], start=True, stop=True)
            return eng.matmul(bank(1), ones32[:, :], bobuf[:, 512:1024], start=True, stop=True)
        v_bo = op("tensor", borep_mms, bump=pe)
        wt("vector", pe, v_bo)
        v_bo2 = op("vector", lambda e: e.tensor_copy(borep[:, :], bank(0, 2)), bump=ve)
        set_guard(0, ve, v_bo2)
        set_guard(1, ve, v_bo2)

        first_wout = True
        x1br_free = [None, None]
        for t in range(NT):
            sl = t % 2
            slx = t % 3
            w = [] if xr2_free[slx] is None else [(ve, xr2_free[slx])]
            s_ld, v_ld = dma("sync", ldx, xr2[:, slx, :],
                             bass.AP(x_in, t * 128 * D, [[D, 128], [1, D]]),
                             waits=w)
            ob0 = 0 if sl == 0 else 2
            wait_bank(ob0)
            wait_bank(ob0 + 1)
            def outmm(eng, t=t, ob0=ob0, fw=first_wout):
                if fw:
                    for s, v in w_wout:
                        eng.wait_ge(s, v)
                for ic in range(8):
                    for dmc in range(2):
                        ins = eng.matmul(
                            bank(ob0 + dmc),
                            bigB[:, ic, t * 128:(t + 1) * 128],
                            w_outb[:, ic, dmc * 512:(dmc + 1) * 512],
                            start=(ic == 0), stop=(ic == 7))
                return ins
            v_om = op("tensor", outmm, bump=pe)
            first_wout = False
            wt("vector", pe, v_om)
            wt("vector", s_ld, v_ld)
            op("vector", lambda e, sl=sl, ob0=ob0: e.tensor_add(
                r2r[:, sl, :], bank(ob0, 2), borep[:, :]))
            drain("vector")
            v_r2 = op("vector", lambda e, sl=sl, slx=slx: e.tensor_add(
                r2r[:, sl, :], r2r[:, sl, :], xr2[:, slx, :]), bump=ve)
            xr2_free[slx] = v_r2
            set_guard(ob0, ve, v_r2)
            set_guard(ob0 + 1, ve, v_r2)
            drain("vector")
            op("vector", lambda e, sl=sl: e.bn_stats(bnr[:, sl, 0, :], r2r[:, sl, 0:512]))
            op("vector", lambda e, sl=sl: e.bn_stats(bnr[:, sl, 1, :], r2r[:, sl, 512:1024]))
            drain("vector")
            v_bn = op("vector", lambda e, sl=sl: e.bn_aggr(mvr[:, sl, :], bnr[:, sl, :, :]), bump=ve)
            wt("scalar", ve, v_bn)
            op("scalar", lambda e, sl=sl: e.activation(rsr[:, sl, :], mvr[:, sl, 1:2], AF.Ln, bias=c_eps[:, :]))
            drain("scalar")
            v_rs = op("scalar", lambda e, sl=sl: e.activation(rsr[:, sl, :], rsr[:, sl, :], AF.Exp, scale=-0.5), bump=ac)
            wt("vector", ac, v_rs)
            if x1br_free[sl] is not None:
                wt("vector", x1br_free[sl][0], x1br_free[sl][1])
            v_x1 = op("vector", lambda e, sl=sl: e.tensor_scalar(
                x1br[:, sl, :], r2r[:, sl, :], mvr[:, sl, 0:1], rsr[:, sl, 0:1],
                ALU.subtract, ALU.mult), bump=ve)
            x1br_free[sl] = dma("sync", stx, bass.AP(x1_scr, t * 128 * D, [[D, 128], [1, D]]),
                                x1br[:, sl, :], waits=[(ve, v_x1)])
            tb = 4 + sl
            wait_bank(tb)
            wt("tensor", ve, v_x1)
            def tr2(eng, sl=sl, tb=tb):
                for c in range(8):
                    ins = eng.matmul(bank16(tb)[:, c * 128:(c + 1) * 128],
                                     x1br[:, sl, c * 128:(c + 1) * 128],
                                     id16[:, :], start=True, stop=True,
                                     is_transpose=True)
                return ins
            v_tr = op("tensor", tr2, bump=pe)
            wt("vector", pe, v_tr)
            v_ev = op("vector", lambda e, t=t, tb=tb: e.tensor_copy(
                bigB[:, :, t * 128:(t + 1) * 128],
                bank16(tb).rearrange("p (c j) -> p c j", c=8)), bump=ve)
            set_guard(tb, ve, v_ev)
            x1T_ready = v_ev
            wait_bank(6)
            wt("tensor", ve, x1T_ready)
            def gatemm(eng, t=t, fw=(t == 0)):
                if fw:
                    for s, v in w_gate:
                        eng.wait_ge(s, v)
                for c in range(8):
                    ins = eng.matmul(bank(6)[:, 0:NE],
                                     bigB[:, c, t * 128:(t + 1) * 128],
                                     gate_wb[:, c, :],
                                     start=(c == 0), stop=(c == 7))
                return ins
            v_gm = op("tensor", gatemm, bump=pe)
            wt("scalar", pe, v_gm)
            v_sx = op("scalar", lambda e, sl=sl: e.activation(
                sexp[:, sl, :], bank(6)[:, 0:NE], AF.Exp), bump=ac)
            set_guard(6, ac, v_sx)
            wt("vector", ac, v_sx)
            op("vector", lambda e, sl=sl: e.reduce_sum(srec[:, sl, :], sexp[:, sl, :], AX.X))
            drain("vector")
            op("vector", lambda e, sl=sl: e.reciprocal(srec[:, sl, :], srec[:, sl, :]))
            drain("vector")
            v_pr = op("vector", lambda e, sl=sl, t=t: e.tensor_scalar(
                probs_tok[:, t, :], sexp[:, sl, :], srec[:, sl, 0:1], None,
                ALU.mult), bump=ve)
            wait_bank(7)
            wt("tensor", ve, v_pr)
            v_pt = op("tensor", lambda e, t=t: e.matmul(
                bank(7)[0:8, 0:128], probs_tok[:, t, :], id32[:, :],
                start=True, stop=True, is_transpose=True), bump=pe)
            wt("vector", pe, v_pt)
            v_pe2 = op("vector", lambda e, t=t: e.tensor_copy(
                probs_et[:, t * 128:(t + 1) * 128], bank(7)[0:8, 0:128]), bump=ve)
            set_guard(7, ve, v_pe2)
        probs_et_done = ve.n
        s2.close()

        s3 = es.enter_context(ExitStack())
        giq = DmaQ(nc, es, "giq", 8)
        tokidu = sb("tokidu", [128, NT], U32, stack=s3)
        s3a = ExitStack()
        probs_rep = sb("probs_rep", [128, T], F32, stack=s3a)
        maskb = sb("maskb", [128, T], BF16, stack=s3a)
        bis = sb("bis", [128, 8], F32, stack=s3a)
        posb = sb("posb", [8, T], F32, stack=s3a)
        dmpb = sb("dmpb", [8, T], F32, stack=s3a)
        scr8 = sb("scr8", [8, T], F32, stack=s3a)
        offrow = sb("offrow", [128, NE], F32, stack=s3a)

        for s_, c_ in phase_waits():
            wt("vector", s_.sem if isinstance(s_, Ctr) else s_, c_)
        for qc in range(NQC):
            b_ = qc % 4
            wait_bank(b_)
            def prep_mm(eng, qc=qc, b_=b_, vv=probs_et_done):
                if qc == 0:
                    eng.wait_ge(ve.sem, vv)
                return eng.matmul(bank(b_), selrep[:, :],
                                  probs_et[:, qc * 512:(qc + 1) * 512],
                                  start=True, stop=True)
            v_mm = op("tensor", prep_mm, bump=pe)
            wt("vector", pe, v_mm)
            v_e = op("vector", lambda e, qc=qc, b_=b_: e.tensor_copy(
                probs_rep[:, qc * 512:(qc + 1) * 512], bank(b_)), bump=ve)
            set_guard(b_, ve, v_e)

        drain("vector")
        op("vector", lambda e: e.memset(bis[:, 0:1], 0.0))
        op("vector", lambda e: e.memset(bis[:, 1:2], 1.0))
        for it in range(BIS_ITERS):
            drain("vector")
            op("vector", lambda e: e.tensor_scalar(
                bis[:, 2:3], bis[:, 0:1], bis[:, 1:2], 0.5, ALU.add, ALU.mult))
            drain("vector")
            op("vector", lambda e: e.tensor_scalar(
                maskb[:, :], probs_rep[:, :], bis[:, 2:3], 0.0,
                ALU.is_ge, ALU.add, accum_out=bis[:, 3:4]))
            drain("vector")
            op("vector", lambda e: e.tensor_scalar(
                bis[:, 4:5], bis[:, 3:4], 1024.0, None, ALU.is_ge))
            op("vector", lambda e: e.tensor_tensor(bis[:, 5:6], bis[:, 2:3], bis[:, 0:1], ALU.subtract))
            drain("vector")
            op("vector", lambda e: e.scalar_tensor_tensor(
                bis[:, 0:1], bis[:, 5:6], bis[:, 4:5], bis[:, 0:1], ALU.mult, ALU.add))
            op("vector", lambda e: e.tensor_tensor(bis[:, 6:7], bis[:, 1:2], bis[:, 2:3], ALU.subtract))
            drain("vector")
            op("vector", lambda e: e.scalar_tensor_tensor(
                bis[:, 1:2], bis[:, 6:7], bis[:, 4:5], bis[:, 2:3], ALU.mult, ALU.add))
        drain("vector")
        op("vector", lambda e: e.tensor_scalar(
            maskb[:, :], probs_rep[:, :], bis[:, 0:1], None, ALU.is_ge))
        drain("vector")
        drain("vector")
        op("vector", lambda e: e.tensor_tensor_scan(
            posb[:, :], maskb[0:8, :], maskb[0:8, :], 0.0, ALU.add, ALU.max))
        drain("vector")
        op("vector", lambda e: e.tensor_tensor(posb[:, :], posb[:, :], maskb[0:8, :], ALU.subtract))
        for s_, c_ in phase_waits():
            wt("gpsimd", s_.sem if isinstance(s_, Ctr) else s_, c_)
        op("gpsimd", lambda e: e.iota(dmpb[:, :], [[0, 32], [1, 128]], base=1024,
                                      channel_multiplier=0,
                                      allow_small_or_imprecise_dtypes=True))
        drain("gpsimd")
        gp_io = Ctr(nc, es, "gpio")
        on("gpsimd", lambda e: e.iota(tokidu[:, :], [[128, NT]], base=0,
                                      channel_multiplier=1,
                                      allow_small_or_imprecise_dtypes=True))
        drain("gpsimd")
        op("gpsimd", lambda e: e.iota(offrow[:, :], [[1152, NE]], base=0,
                                      channel_multiplier=0,
                                      allow_small_or_imprecise_dtypes=True), bump=gp_io)
        drain("vector")
        op("vector", lambda e: e.tensor_scalar(
            scr8[:, :], posb[:, :], 1023.0, None, ALU.is_le))
        drain("vector")
        op("vector", lambda e: e.tensor_tensor(scr8[:, :], scr8[:, :], maskb[0:8, :], ALU.mult))
        wt("vector", gp_io, gp_io.n)
        op("vector", lambda e: e.tensor_tensor(posb[:, :], posb[:, :], dmpb[:, :], ALU.subtract))
        drain("vector")
        op("vector", lambda e: e.tensor_tensor(posb[:, :], posb[:, :], scr8[:, :], ALU.mult))
        drain("vector")
        v_slot = op("vector", lambda e: e.tensor_tensor(posb[:, :], posb[:, :], dmpb[:, :], ALU.add), bump=ve)
        early_ts = {0: {}, 1: {}}
        wt("tensor", ve, v_slot)
        for t in range(NT):
            b_ = t % 2
            wait_bank(b_)
            v_tr = op("tensor", lambda e, t=t, b_=b_: e.matmul(
                bank(b_)[:, 0:8], posb[0:8, t * 128:(t + 1) * 128], id32[0:8, 0:8],
                start=True, stop=True, is_transpose=True), bump=pe)
            wt("vector", pe, v_tr)
            op("vector", lambda e, t=t, b_=b_: e.tensor_tensor(
                bank(b_)[:, 0:8], bank(b_)[:, 0:8], offrow[:, :], ALU.add))
            drain("vector")
            v_ev = op("vector", lambda e, t=t, b_=b_: e.tensor_copy(
                slot_tok[:, t, :], bank(b_)[:, 0:8]), bump=ve)
            set_guard(b_, ve, v_ev)
            for e0_ in (0, 1):
                semh, tgt, pre = giq.issue()
                def fts0(eng, t=t, e_=e0_, s=semh, pre=pre, vv=v_ev):
                    if pre is not None:
                        eng.wait_ge(pre[0], pre[1])
                    eng.wait_ge(ve.sem, vv)
                    eng.indirect_dma_start(
                        bass.AP(toksl_scr, 0, [[1, NE * 1152], [1, 1]]),
                        IndirectOffsetOnAxis(slot_tok[:, t, e_:e_ + 1], 0),
                        tokidu[:, t:t + 1], None,
                        bounds_check=bc_regs[e_], oob_is_err=False,
                    ).then_inc(s, 16)
                on("gpsimd", fts0)
                early_ts[e0_][semh] = tgt
        slot_done = ve.n
        meta_st = dma("gpsimd", acq,
                      bass.AP(meta_scr, 0, [[NE, 128], [NE * 128, NT], [1, NE]]),
                      probs_tok[:, :, :], waits=[(ve, ve.n)])
        s3a.close()

        spill_done = stx.total_waits() + [meta_st]
        toksl_ready = {0: list(early_ts[0].items()),
                       1: list(early_ts[1].items())}

        def emit_tokscat(e_):
            tgts = {}
            for t in range(NT):
                w = []
                if e_ == 0 and t == 0:
                    w.append((ve, slot_done))
                semh, tgt, pre = giq.issue()
                def fts(eng, t=t, e_=e_, s=semh, pre=pre, w=mkwaits(w)):
                    if pre is not None:
                        eng.wait_ge(pre[0], pre[1])
                    for ws, wv in w:
                        eng.wait_ge(ws, wv)
                    eng.indirect_dma_start(
                        bass.AP(toksl_scr, 0, [[1, NE * 1152], [1, 1]]),
                        IndirectOffsetOnAxis(slot_tok[:, t, e_:e_ + 1], 0),
                        tokidu[:, t:t + 1], None,
                        bounds_check=bc_regs[e_], oob_is_err=False,
                    ).then_inc(s, 16)
                on("gpsimd", fts)
                tgts[semh] = tgt
            toksl_ready[e_] = list(tgts.items())

        x1l = sb("x1l", [128, 2, D], BF16, stack=s3)
        r3r = sb("r3r", [128, 2, D], F32, stack=s3)
        outr = sb("outr", [128, 2, D], F32, stack=s3)
        x1l_free = [None, None]
        outr_free = [None, None]
        s3b = es.enter_context(ExitStack())
        w1full = sb("w1full", [128, 8, HID], BF16, stack=s3b)
        w2s = sb("w2s", [128, 2, 4, D], BF16, stack=s3b)
        yscat = sb("yscat", [128, 2, D], F32, stack=s3b)
        probg = sb("probg", [128, 2, 8, NE], BF16, stack=s3b)
        probf = sb("probf", [128, 2, 8, NE], F32, stack=s3b)
        tokiu = sb("tokiu", [128, 2, 8, 1], U32, stack=s3b)
        w2_free = [None, None]
        yscat_free = [None, None]
        mload_free = [None, None]
        tokiu_free = [None, None]
        ysc = bigB[:, 0, :].rearrange("p (c j) -> p c j", c=8)

        def hT_v(hc):
            return bigB[:, hc // 8, (hc % 8) * 512:(hc % 8) * 512 + 512]

        def xetok_v(r):
            return bigB[:, 4 + r, :]

        def xeT_v(r, cdm):
            return bigB[:, 6 + r, cdm * 512:(cdm + 1) * 512]

        def w1_ap(e_):
            return bass.AP(w1bf, e_ * D * HID, [[HID, 128], [HID * 128, 8], [1, HID]])

        def w2_ap(e_, hc0):
            return bass.AP(w2bf, e_ * HID * D + hc0 * 128 * D,
                           [[D, 128], [128 * D, 4], [1, D]])

        w1_tgt = dma("sync", wfq, w1full[:, :, :], w1_ap(0),
                     waits=phase_waits() + [wconv[0][0]])
        xetok_tgt = {}
        tok_ring_free = [None, None]
        xeT_ring_free = [None, None]
        first_scat = [True]
        first_pg = [True]
        cur_scats = [[]]
        scats_all = {}

        def emit_xetok_loads(e_):
            r2 = e_ % 2
            wti = list(toksl_ready[e_])
            if e_ - 2 >= 0:
                wti += scats_all[e_ - 2]
            tld = []
            for i in range(8):
                tld.append(dma("sync", ldx, tokiu[:, r2, i, :],
                               bass.AP(toksl_scr, e_ * 1152 + i * 128,
                                       [[1, 128], [1, 1]]),
                               waits=(wti if i == 0 else [])))
            for blk in range(2):
                r = blk % 2
                gl = []
                for g in range(4):
                    i = blk * 4 + g
                    gw = [tld[i]]
                    if e_ == 0 and i == 0:
                        gw += list(spill_done)
                    if g == 0 and tok_ring_free[r] is not None:
                        gw.append((pe.sem, tok_ring_free[r]))
                    semh, tgt, pre = giq.issue()
                    def fxg(eng, g=g, r=r, r2=r2, i=i, s=semh, pre=pre,
                            gw=mkwaits(gw)):
                        if pre is not None:
                            eng.wait_ge(pre[0], pre[1])
                        for ws, wv in gw:
                            eng.wait_ge(ws, wv)
                        eng.indirect_dma_start(
                            bass.AP(bigB, (4 + r) * T + g * 1024,
                                    [[8 * T, 128], [1, D]]),
                            None,
                            bass.AP(x1_scr, 0, [[D, T], [1, D]]),
                            IndirectOffsetOnAxis(tokiu[:, r2, i, 0:1], 0),
                        ).then_inc(s, 16)
                    on("gpsimd", fxg)
                    gl.append((semh, tgt))
                xetok_tgt[(e_, r)] = gl
            gsems = []
            for i in range(8):
                semh, tgt, pre = giq.issue()
                gw = [tld[i]]
                if first_pg[0]:
                    first_pg[0] = False
                    gw.append(meta_st)
                if mload_free[r2] is not None:
                    gw.append((mload_free[r2][0].sem, mload_free[r2][1]))
                def fpg(eng, i=i, r2=r2, s=semh, pre=pre, gw=mkwaits(gw)):
                    if pre is not None:
                        eng.wait_ge(pre[0], pre[1])
                    for ws, wv in gw:
                        eng.wait_ge(ws, wv)
                    eng.indirect_dma_start(
                        probg[:, r2, i, :], None,
                        bass.AP(meta_scr, 0, [[NE, T], [1, NE]]),
                        IndirectOffsetOnAxis(tokiu[:, r2, i, 0:1], 0),
                    ).then_inc(s, 16)
                on("gpsimd", fpg)
                gsems.append((semh, tgt))
            for s_, v_ in gsems:
                wt("vector", s_, v_)
            drain("vector")
            op("vector", lambda e, r2=r2: e.tensor_copy(
                probf[:, r2, :, :], probg[:, r2, :, :]), bump=ve)
            drain("vector")

        emit_xetok_loads(0)

        for e_ in range(NE):
            if e_ + 2 < NE:
                emit_tokscat(e_ + 2)
            for blk in range(2):
                r = blk % 2
                for sgh, sgt in xetok_tgt[(e_, r)]:
                    wt("tensor", sgh, sgt)
                for g in range(4):
                    tb_ = 4 + (g % 2)
                    wait_bank(tb_)
                    def xtr(eng, r=r, g=g, tb_=tb_):
                        for cdm in range(8):
                            ins = eng.matmul(
                                bank16(tb_)[:, cdm * 128:(cdm + 1) * 128],
                                xetok_v(r)[:, g * 1024 + cdm * 128:g * 1024 + (cdm + 1) * 128],
                                id16[:, :], start=True, stop=True,
                                is_transpose=True)
                        return ins
                    v_xt = op("tensor", xtr, bump=pe)
                    wt("vector", pe, v_xt)
                    if g == 0 and xeT_ring_free[r] is not None:
                        wt("vector", pe, xeT_ring_free[r])
                    def xev(eng, r=r, g=g, tb_=tb_):
                        return eng.tensor_copy(
                            bass.AP(bigB, (6 + r) * T + g * 128,
                                    [[8 * T, 128], [512, 8], [1, 128]]),
                            bank16(tb_).rearrange("p (c j) -> p c j", c=8))
                    v_xe = op("vector", xev, bump=ve)
                    set_guard(tb_, ve, v_xe)
                xeT_done = ve.n
                tok_ring_free[r] = pe.n
                if blk == 1 and e_ + 1 < NE:
                    emit_xetok_loads(e_ + 1)
                for hc in range(NHC):
                    b_ = hc % 2
                    wait_bank(b_)
                    def hmm(eng, hc=hc, r=r, b_=b_,
                            w1w=(w1_tgt if (blk == 0 and hc == 0) else None),
                            xw=(xeT_done if hc == 0 else None)):
                        if w1w is not None:
                            eng.wait_ge(w1w[0], w1w[1])
                        if xw is not None:
                            eng.wait_ge(ve.sem, xw)
                        for c in range(8):
                            ins = eng.matmul(
                                bank(b_),
                                w1full[:, c, hc * 128:(hc + 1) * 128],
                                xeT_v(r, c),
                                start=(c == 0), stop=(c == 7))
                        return ins
                    v_h = op("tensor", hmm, bump=pe)
                    wt("scalar", pe, v_h)
                    v_g = op("scalar", lambda e, hc=hc, b_=b_: e.activation(
                        hT_v(hc), bank(b_), AF.Gelu), bump=ac)
                    set_guard(b_, ac, v_g)
                xeT_ring_free[r] = pe.n
                gelu_all = ac.n
                for i in range(8):
                    wait_bank(i)
                wt("tensor", ac, gelu_all)
                v_y = None
                for hcg in range(8):
                    wsl = hcg % 2
                    w2w = []
                    if w2_free[wsl] is not None:
                        w2w.append((pe, w2_free[wsl]))
                    if hcg == 0:
                        w2w.append(wconv[e_][1])
                    w2_tgt = dma("sync", wfq if hcg % 2 == 0 else wfq2,
                                 w2s[:, wsl, :, :],
                                 w2_ap(e_, hcg * 4), waits=w2w)
                    for hci in range(4):
                        hc = hcg * 4 + hci
                        def ymm(eng, hc=hc, wsl=wsl, hci=hci,
                                w2w_=(w2_tgt if hci == 0 else None)):
                            if w2w_ is not None:
                                eng.wait_ge(w2w_[0], w2w_[1])
                            for cc in range(8):
                                ins = eng.matmul(
                                    bank(cc),
                                    w2s[:, wsl, hci, cc * 128:(cc + 1) * 128],
                                    hT_v(hc),
                                    start=(hc == 0), stop=(hc == NHC - 1))
                            return ins
                        v_y = op("tensor", ymm, bump=pe)
                    w2_free[wsl] = pe.n
                if blk == 1 and e_ + 1 < NE:
                    w1_tgt = dma("sync", wfq, w1full[:, :, :], w1_ap(e_ + 1),
                                 waits=[(pe, pe.n), wconv[e_ + 1][0]])
                wt("vector", pe, v_y)
                for cc in range(8):
                    v_ysc = op("vector", lambda e, cc=cc: e.tensor_copy(
                        ysc[:, cc, :], bank(cc)), bump=ve)
                    set_guard(cc, ve, v_ysc)
                ysc_done = ve.n
                for stt in range(4):
                    i = blk * 4 + stt
                    tb_ = stt % 2
                    sl2 = i % 2
                    r2 = e_ % 2
                    wait_bank(tb_)
                    wt("tensor", ve, ysc_done)
                    def ytr(eng, stt=stt, tb_=tb_):
                        for cc in range(8):
                            ins = eng.matmul(
                                bank16(tb_)[:, cc * 128:(cc + 1) * 128],
                                ysc[:, cc, stt * 128:(stt + 1) * 128],
                                id16[:, :], start=True, stop=True,
                                is_transpose=True)
                        return ins
                    v_t = op("tensor", ytr, bump=pe)
                    wt("vector", pe, v_t)
                    if yscat_free[sl2] is not None:
                        wt("vector", yscat_free[sl2][0], yscat_free[sl2][1])
                    drain("vector")
                    v_yt = op("vector", lambda e, i=i, sl2=sl2, r2=r2, tb_=tb_,
                              e_=e_: e.tensor_scalar(
                                  yscat[:, sl2, :], bank16(tb_),
                                  probf[:, r2, i, e_:e_ + 1], None, ALU.mult), bump=ve)
                    set_guard(tb_, ve, v_yt)
                    semh, tgt, pre = ysq.issue()
                    sw = [(ve.sem, v_yt)]
                    if i == 0:
                        sw += list(mkwaits(cur_scats[0]))
                        cur_scats[0] = []
                    if first_scat[0]:
                        first_scat[0] = False
                        sw += list(mkwaits(r3z_waits))
                    def fsc(eng, i=i, sl2=sl2, r2=r2, s=semh, pre=pre,
                            sw=tuple(sw)):
                        if pre is not None:
                            eng.wait_ge(pre[0], pre[1])
                        for ws, wv in sw:
                            eng.wait_ge(ws, wv)
                        eng.indirect_dma_start(
                            bass.AP(r3_scr, 0, [[D, T], [1, D]]),
                            IndirectOffsetOnAxis(tokiu[:, r2, i, 0:1], 0),
                            yscat[:, sl2, :], None,
                            compute_op=ALU.add,
                        ).then_inc(s, 16)
                    on("gpsimd", fsc)
                    yscat_free[sl2] = (semh, tgt)
                    cur_scats[0].append((semh, tgt))
                    scats_all.setdefault(e_, []).append((semh, tgt))
                    if blk == 1 and stt == 3:
                        mload_free[r2] = (ve, v_yt)
        s3b.close()

        r3_parts = stx.total_waits()
        scat_done = ysq.total_waits()
        r3f = [None, None]
        x1lf = [None, None]
        for t in range(NT):
            sl = t % 2
            w2_ = []
            if t == 0:
                w2_ += list(r3_parts) + list(scat_done)
            if r3f[sl] is not None:
                w2_.append(r3f[sl])
            s_b, v_b = dma("sync", ldx if t % 2 == 0 else wfq2, r3r[:, sl, :],
                           bass.AP(r3_scr, t * 128 * D, [[D, 128], [1, D]]),
                           waits=w2_)
            wx_ = []
            if t == 0:
                wx_ += list(r3_parts)
            if x1lf[sl] is not None:
                wx_.append(x1lf[sl])
            s_x, v_x = dma("scalar", mvq, x1l[:, sl, :],
                           bass.AP(x1_scr, t * 128 * D, [[D, 128], [1, D]]),
                           waits=wx_)
            wt("vector", s_b, v_b)
            wt("vector", s_x, v_x)
            drain("vector")
            v_add = op("vector", lambda e, sl=sl: e.tensor_tensor(
                r3r[:, sl, :], r3r[:, sl, :], x1l[:, sl, :], ALU.add), bump=ve)
            x1lf[sl] = (ve, v_add)
            drain("vector")
            op("vector", lambda e, sl=sl: e.bn_stats(bnr[:, sl, 0, :], r3r[:, sl, 0:512]))
            op("vector", lambda e, sl=sl: e.bn_stats(bnr[:, sl, 1, :], r3r[:, sl, 512:1024]))
            drain("vector")
            v_bn = op("vector", lambda e, sl=sl: e.bn_aggr(mvr[:, sl, :], bnr[:, sl, :, :]), bump=ve)
            wt("scalar", ve, v_bn)
            op("scalar", lambda e, sl=sl: e.activation(rsr[:, sl, :], mvr[:, sl, 1:2], AF.Ln, bias=c_eps[:, :]))
            drain("scalar")
            v_rs = op("scalar", lambda e, sl=sl: e.activation(rsr[:, sl, :], rsr[:, sl, :], AF.Exp, scale=-0.5), bump=ac)
            wt("vector", ac, v_rs)
            if outr_free[sl] is not None:
                wt("vector", outr_free[sl][0], outr_free[sl][1])
            v_o = op("vector", lambda e, sl=sl: e.tensor_scalar(
                outr[:, sl, :], r3r[:, sl, :], mvr[:, sl, 0:1], rsr[:, sl, 0:1],
                ALU.subtract, ALU.mult), bump=ve)
            r3f[sl] = (ve, v_o)
            so, vo = dma("sync", stx if sl == 0 else wfq,
                         bass.AP(out, t * 128 * D, [[D, 128], [1, D]]),
                         outr[:, sl, :], waits=[(ve, v_o)])
            outr_free[sl] = (so, vo)
        s3.close()

        for s, c in stx.total_waits() + wfq.total_waits():
            wt("sync", s, c)

        with nc.Block() as block:
            @block.sync
            def _(eng):
                for f in plan["sync"]:
                    f(eng)

            @block.gpsimd
            def _(eng):
                for f in plan["gpsimd"]:
                    f(eng)

            @block.vector
            def _(eng):
                for f in plan["vector"]:
                    f(eng)

            @block.scalar
            def _(eng):
                for f in plan["scalar"]:
                    f(eng)

            @block.tensor
            def _(eng):
                for f in plan["tensor"]:
                    f(eng)

    return nc


_NC_CACHE = None


def kernel(L0_attn_x, L0_attn_ln_g, L0_attn_ln_b, L0_attn_w_qkv, L0_attn_w_out,
           L0_attn_b_out, L0_ffn_gate_w, L0_ffn_w1, L0_ffn_w2):
    global _NC_CACHE
    x = np.ascontiguousarray(np.asarray(L0_attn_x, dtype=np.float32))
    ln_g = np.ascontiguousarray(np.asarray(L0_attn_ln_g, dtype=np.float32)).reshape(1, D)
    ln_b = np.ascontiguousarray(np.asarray(L0_attn_ln_b, dtype=np.float32)).reshape(1, D)
    wqkv = np.ascontiguousarray(np.asarray(L0_attn_w_qkv, dtype=np.float32))
    wout = np.ascontiguousarray(np.asarray(L0_attn_w_out, dtype=np.float32))
    bout = np.ascontiguousarray(np.asarray(L0_attn_b_out, dtype=np.float32)).reshape(1, D)
    gw = np.ascontiguousarray(np.asarray(L0_ffn_gate_w, dtype=np.float32))
    w1a = np.ascontiguousarray(np.asarray(L0_ffn_w1, dtype=np.float32))
    w2a = np.ascontiguousarray(np.asarray(L0_ffn_w2, dtype=np.float32))

    if _NC_CACHE is None:
        _NC_CACHE = build()
    nc = _NC_CACHE

    in_maps = []
    for i in range(8):
        in_maps.append({
            "x": x[i], "ln_g": ln_g, "ln_b": ln_b, "w_qkv": wqkv,
            "w_out": wout, "b_out": bout, "gate_w": gw, "w1": w1a, "w2": w2a,
        })
    res = run_bass_kernel_spmd(nc, in_maps, core_ids=list(range(8)))
    return np.stack([r["out"] for r in res.results], axis=0)


# version 21 (final)
# speedup vs baseline: 1.0398x; 1.0398x over previous
import numpy as np
from contextlib import ExitStack

import concourse.bass as bass
import concourse.mybir as mybir
from concourse.bass_utils import run_bass_kernel_spmd
from concourse.bass_core_rust import IndirectOffsetOnAxis
from concourse.library_config import local_scatter as LS_LIB, standard as STD_LIB

F32 = mybir.dt.float32
BF16 = mybir.dt.bfloat16
AF = mybir.ActivationFunctionType
ALU = mybir.AluOpType
AX = mybir.AxisListType
I16 = mybir.dt.int16
U32 = mybir.dt.uint32

T = 4096
D = 1024
NT = T // 128
NPAIR = 8
NE = 8
HID = 4096
NHC = HID // 128
NQC = T // 512
NKC = T // 128
NE_RUN = NE
DEBUG_TAPS = False
TB_RUN = 8
BIS_ITERS = 34

ENGS = ("sync", "gpsimd", "vector", "scalar", "tensor")


class Ctr:
    def __init__(self, nc, es, name):
        self.sem = es.enter_context(nc.semaphore(name))
        self.n = 0

    def bump(self):
        self.n += 1
        return self.n


class DmaQ:

    def __init__(self, nc, es, name, depth=6):
        self.sems = [es.enter_context(nc.semaphore(f"{name}{i}")) for i in range(depth)]
        self.cnt = [0] * depth
        self.i = 0
        self.depth = depth

    def issue(self):
        k = self.i % self.depth
        pre = (self.sems[k], self.cnt[k]) if self.cnt[k] > 0 else None
        self.cnt[k] += 16
        self.i += 1
        return self.sems[k], self.cnt[k], pre

    def total_waits(self):
        return [(s, c) for s, c in zip(self.sems, self.cnt) if c > 0]


def build():
    nc = bass.Bass("TRN2", target_bir_lowering=False)

    x_in = nc.dram_tensor("x", [T, D], F32, kind="ExternalInput")
    ln_g = nc.dram_tensor("ln_g", [1, D], F32, kind="ExternalInput").ap()
    ln_b = nc.dram_tensor("ln_b", [1, D], F32, kind="ExternalInput").ap()
    w_qkv = nc.dram_tensor("w_qkv", [D, 3 * D], F32, kind="ExternalInput")
    w_out = nc.dram_tensor("w_out", [D, D], F32, kind="ExternalInput")
    b_out = nc.dram_tensor("b_out", [1, D], F32, kind="ExternalInput").ap()
    gate_w = nc.dram_tensor("gate_w", [D, NE], F32, kind="ExternalInput")
    w1 = nc.dram_tensor("w1", [NE, D, HID], F32, kind="ExternalInput")
    w2 = nc.dram_tensor("w2", [NE, HID, D], F32, kind="ExternalInput")
    out = nc.dram_tensor("out", [T, D], F32, kind="ExternalOutput")

    x1_scr = nc.dram_tensor("x1_scr", [T, D], BF16, kind="Internal")
    if DEBUG_TAPS:
        dbg_h = nc.dram_tensor("dbg_h", [128, NHC, 512], BF16, kind="Internal")
        dbg_y = nc.dram_tensor("dbg_y", [128, 8, 512], BF16, kind="Internal")
    meta_scr = nc.dram_tensor("meta_scr", [T, NE], BF16, kind="Internal")
    toksl_scr = nc.dram_tensor("toksl_scr", [NE * 1152, 1], U32, kind="Internal")
    r3_scr = nc.dram_tensor("r3_scr", [T, D], F32, kind="Internal")
    w1bf = nc.dram_tensor("w1bf", [NE, D, HID], BF16, kind="Internal")
    w2bf = nc.dram_tensor("w2bf", [NE, HID, D], BF16, kind="Internal")

    plan = {e: [] for e in ENGS}

    def on(e, fn):
        plan[e].append(fn)

    def wt(e, sem, val):
        if isinstance(sem, Ctr):
            sem = sem.sem
        plan[e].append(lambda eng, s=sem, v=val: eng.wait_ge(s, v))

    es = ExitStack()
    with es:
        pe = Ctr(nc, es, "pe")
        ve = Ctr(nc, es, "ve")
        ac = Ctr(nc, es, "ac")
        ldx = DmaQ(nc, es, "ldx", 6)
        stx = DmaQ(nc, es, "stx", 6)
        ldw = DmaQ(nc, es, "ldw", 8)
        acq = DmaQ(nc, es, "acq", 4)
        mvq = DmaQ(nc, es, "mvq", 4)
        ysq = DmaQ(nc, es, "ysq", 8)
        wcq = [DmaQ(nc, es, f"wc{i}", 2) for i in range(2)]
        wfq = DmaQ(nc, es, "wfq", 8)
        wfq2 = DmaQ(nc, es, "wfq2", 4)

        def sb(name, shape, dt, stack=es):
            return stack.enter_context(nc.sbuf_tensor(name, shape, dt))

        c_eps = sb("c_eps", [128, 1], F32)
        c_m20 = sb("c_m20", [128, 1], F32)
        ones32 = sb("ones32", [1, 128], F32)
        ones8 = sb("ones8", [8, 128], BF16)
        id16 = sb("id16", [128, 128], BF16)
        id32 = sb("id32", [128, 128], F32)
        selrep = sb("selrep", [8, 128], F32)
        zt = sb("zt", [128, 512], F32)
        gate_wb = sb("gate_wb", [128, 8, NE], BF16)
        bnr = sb("bnr", [128, 2, 2, 6], F32)
        mvr = sb("mvr", [128, 2, 2], F32)
        rsr = sb("rsr", [128, 2, 1], F32)
        bigB = sb("bigB", [128, 8, T], BF16)

        ps = es.enter_context(nc.psum_tensor("ps", [128, 4096], F32))

        def bank(i, n=1):
            return ps[:, i * 512:(i + n) * 512]

        def bank16(i):
            return ps[:, i * 512:(i + 1) * 512].bitcast(BF16)

        bank_guard = {i: None for i in range(8)}

        def wait_bank(i, eng="tensor"):
            g = bank_guard[i]
            if g is not None:
                wt(eng, g[0], g[1])
                bank_guard[i] = None

        def set_guard(i, ctr, val):
            bank_guard[i] = (ctr.sem, val)

        def mkwaits(waits):
            res = []
            for s, v in waits:
                res.append((s.sem if isinstance(s, Ctr) else s, v))
            return tuple(res)

        def dma(q_eng, stream, out_ap, in_ap, waits=()):
            semh, tgt, pre = stream.issue()
            w = mkwaits(waits)
            def f(eng, o=out_ap, i=in_ap, w=w, s=semh, pre=pre):
                if pre is not None:
                    eng.wait_ge(pre[0], pre[1])
                for ws, wv in w:
                    eng.wait_ge(ws, wv)
                eng.dma_start(out=o, in_=i).then_inc(s, 16)
            on(q_eng, f)
            return semh, tgt

        def dma_acc(stream, out_ap, in_ap, waits=(), accum=True):
            semh, tgt, pre = stream.issue()
            w = mkwaits(waits)
            aop = ALU.add if accum else ALU.bypass
            def f(eng, o=out_ap, i=in_ap, w=w, s=semh, aop=aop, pre=pre):
                if pre is not None:
                    eng.wait_ge(pre[0], pre[1])
                for ws, wv in w:
                    eng.wait_ge(ws, wv)
                eng.dma_start(out=o, in_=i, accum_op=aop).then_inc(s, 16)
            on("gpsimd", f)
            return semh, tgt

        def op(e, fn, bump=None):
            if bump is None:
                on(e, fn)
                return None
            v = bump.bump()
            def f(eng, fn=fn, s=bump.sem):
                fn(eng).then_inc(s, 1)
            on(e, f)
            return v

        def drain(e):
            on(e, lambda eng: eng.drain())

        def phase_waits():
            w = [(pe, pe.n), (ve, ve.n), (ac, ac.n)]
            for q in (ldx, stx, ldw, acq, mvq, wfq, wfq2, *wcq):
                w += q.total_waits()
            return w

        for ap_, val in ((c_eps[:, :], 1e-5), (c_m20[:, :], -20.0),
                         (ones32[:, :], 1.0), (ones8[:, :], 1.0),
                         (id16[:, :], 1.0), (id32[:, :], 1.0)):
            on("gpsimd", lambda e, a=ap_, v=val: e.memset(a, v))
        gp0 = Ctr(nc, es, "gp0")
        op("gpsimd", lambda e: e.engine_nop(), bump=gp0)
        wt("gpsimd", gp0, gp0.n)
        on("gpsimd", lambda e: e.affine_select(
            id16[:, :], id16[:, :], [[1, 128]], ALU.is_equal, 0.0,
            base=0, channel_multiplier=-1))
        on("gpsimd", lambda e: e.affine_select(
            id32[:, :], id32[:, :], [[1, 128]], ALU.is_equal, 0.0,
            base=0, channel_multiplier=-1))
        on("gpsimd", lambda e: e.memset(zt[:, :], 0.0))
        bc_regs = [nc.gpsimd.alloc_register(f"bce{e_}") for e_ in range(NE)]
        for e_ in range(NE):
            on("gpsimd", lambda e, e_=e_: e.reg_mov(bc_regs[e_], e_ * 1152 + 1023))
        on("gpsimd", lambda e: e.memset(selrep[:, :], 1.0))
        gp1 = Ctr(nc, es, "gp1")
        op("gpsimd", lambda e: e.engine_nop(), bump=gp1)
        wt("gpsimd", gp1, gp1.n)
        on("gpsimd", lambda e: e.affine_select(
            selrep[:, :], selrep[:, :], [[0, 16], [1, 8]], ALU.is_equal, 0.0,
            base=0, channel_multiplier=-1))
        gp_consts = Ctr(nc, es, "gpc")
        op("gpsimd", lambda e: e.engine_nop(), bump=gp_consts)

        dma("gpsimd", ldw, gate_wb[:, :, :],
            bass.AP(gate_w, 0, [[NE, 128], [NE * 128, 8], [1, NE]]))
        w_gate = ldw.total_waits()
        drain("gpsimd")
        for zi in range(NT):
            for h in range(2):
                dma_acc(acq, bass.AP(r3_scr, zi * 128 * D + h * 512,
                                     [[D, 128], [1, 512]]),
                        zt[:, :], accum=False)
        r3z_waits = acq.total_waits()
        wconv = {}

        def emit_wconv(e_):
            a = dma("gpsimd", wcq[0],
                    bass.AP(w1bf, e_ * D * HID, [[HID, D], [2048, 2], [1, 2048]]),
                    bass.AP(w1, e_ * D * HID, [[HID, D], [2048, 2], [1, 2048]]))
            b = dma("gpsimd", wcq[1],
                    bass.AP(w2bf, e_ * HID * D, [[D, HID], [512, 2], [1, 512]]),
                    bass.AP(w2, e_ * HID * D, [[D, HID], [512, 2], [1, 512]]))
            wconv[e_] = [a, b]

        s1 = es.enter_context(ExitStack())
        xnT = sb("xnT", [128, 8, T], BF16, stack=s1)
        s1a = ExitStack()
        g_rep = sb("g_rep", [128, D], F32, stack=s1a)
        b_rep = sb("b_rep", [128, D], F32, stack=s1a)
        xnr = sb("xnr", [128, 2, D], F32, stack=s1a)
        xnbr = sb("xnbr", [128, 2, D], BF16, stack=s1a)
        xr1 = sb("xr", [128, 3, D], F32, stack=s1a)
        gbuf = sb("gbuf", [1, D], F32, stack=s1a)
        bbuf = sb("bbuf", [1, D], F32, stack=s1a)
        dma("sync", ldx, gbuf[:, :], ln_g)
        dma("sync", ldx, bbuf[:, :], ln_b)
        ld_params = ldx.total_waits()

        wt("tensor", gp_consts, gp_consts.n)
        def grep_mms(eng):
            for s, v in ld_params:
                eng.wait_ge(s, v)
            eng.matmul(bank(0), ones32[:, :], gbuf[:, 0:512], start=True, stop=True)
            eng.matmul(bank(1), ones32[:, :], gbuf[:, 512:1024], start=True, stop=True)
            eng.matmul(bank(2), ones32[:, :], bbuf[:, 0:512], start=True, stop=True)
            return eng.matmul(bank(3), ones32[:, :], bbuf[:, 512:1024], start=True, stop=True)
        v_grep = op("tensor", grep_mms, bump=pe)
        wt("vector", pe, v_grep)
        op("vector", lambda e: e.tensor_copy(g_rep[:, :], bank(0, 2)))
        v_reps = op("vector", lambda e: e.tensor_copy(b_rep[:, :], bank(2, 2)), bump=ve)
        for i in range(4):
            set_guard(i, ve, v_reps)

        xr1_free = [None, None, None]
        for t in range(NT):
            sl = t % 2
            slx = t % 3
            w = [] if xr1_free[slx] is None else [(ve, xr1_free[slx])]
            s_ld, v_ld = dma("sync", ldx, xr1[:, slx, :],
                             bass.AP(x_in, t * 128 * D, [[D, 128], [1, D]]),
                             waits=w)
            wt("vector", s_ld, v_ld)
            op("vector", lambda e, sl=sl, slx=slx: e.bn_stats(bnr[:, sl, 0, :], xr1[:, slx, 0:512]))
            op("vector", lambda e, sl=sl, slx=slx: e.bn_stats(bnr[:, sl, 1, :], xr1[:, slx, 512:1024]))
            drain("vector")
            v_bn = op("vector", lambda e, sl=sl: e.bn_aggr(mvr[:, sl, :], bnr[:, sl, :, :]), bump=ve)
            wt("scalar", ve, v_bn)
            op("scalar", lambda e, sl=sl: e.activation(rsr[:, sl, :], mvr[:, sl, 1:2], AF.Ln, bias=c_eps[:, :]))
            drain("scalar")
            v_rs = op("scalar", lambda e, sl=sl: e.activation(rsr[:, sl, :], rsr[:, sl, :], AF.Exp, scale=-0.5), bump=ac)
            wt("vector", ac, v_rs)
            op("vector", lambda e, sl=sl, slx=slx: e.tensor_scalar(
                xnr[:, sl, :], xr1[:, slx, :], mvr[:, sl, 0:1], rsr[:, sl, 0:1],
                ALU.subtract, ALU.mult))
            drain("vector")
            op("vector", lambda e, sl=sl: e.tensor_mul(xnr[:, sl, :], xnr[:, sl, :], g_rep[:, :]))
            drain("vector")
            v_xnb = op("vector", lambda e, sl=sl: e.tensor_tensor(
                xnbr[:, sl, :], xnr[:, sl, :], b_rep[:, :], ALU.add), bump=ve)
            xr1_free[slx] = v_xnb
            tb = 4 + sl
            wait_bank(tb)
            wt("tensor", ve, v_xnb)
            def tr_all(eng, sl=sl, tb=tb):
                for c in range(8):
                    ins = eng.matmul(bank16(tb)[:, c * 128:(c + 1) * 128],
                                     xnbr[:, sl, c * 128:(c + 1) * 128],
                                     id16[:, :], start=True, stop=True,
                                     is_transpose=True)
                return ins
            v_tr = op("tensor", tr_all, bump=pe)
            wt("vector", pe, v_tr)
            v_ev = op("vector", lambda e, t=t, tb=tb: e.tensor_copy(
                xnT[:, :, t * 128:(t + 1) * 128],
                bank16(tb).rearrange("p (c j) -> p c j", c=8)), bump=ve)
            set_guard(tb, ve, v_ev)
        s1a.close()

        s1b = es.enter_context(ExitStack())
        wqs = sb("wqs", [128, 2, 8, 128], BF16, stack=s1b)
        wks = sb("wks", [128, 2, 8, 128], BF16, stack=s1b)
        wvs = sb("wvs", [128, 2, 8, 128], BF16, stack=s1b)
        qT = sb("qT", [128, T], BF16, stack=s1b)
        kT = sb("kT", [128, T], BF16, stack=s1b)
        vauga = sb("vauga", [128, NKC, 128], BF16, stack=s1b)
        vaugb = sb("vaugb", [128, NKC, 128], BF16, stack=s1b)
        pt = sb("pt", [128, 2, 3, 512], BF16, stack=s1b)
        rz_src = sb("rz_src", [128, 2, 512], F32, stack=s1b)
        rz_dst = sb("rz_dst", [128, 2, 512], F32, stack=s1b)
        onorm = sb("onorm", [128, 2, 2, 512], F32, stack=s1b)

        def w_slice_ap(col0):
            return bass.AP(w_qkv, col0, [[3 * D, 128], [3 * D * 128, 8], [1, 128]])

        wpair_tgt = [None] * NPAIR

        qkv_done = {}

        def load_pair_w(p):
            slb = p % 2
            w = []
            if slb in qkv_done:
                w.append((pe, qkv_done[slb]))
            else:
                w += [(pe, pe.n), (ve, ve.n), (ac, ac.n)]
            a = dma("gpsimd", ldw, wqs[:, slb, :, :], w_slice_ap(p * 128), waits=w)
            b = dma("gpsimd", ldw, wks[:, slb, :, :], w_slice_ap(D + p * 128))
            c = dma("gpsimd", ldw, wvs[:, slb, :, :], w_slice_ap(2 * D + p * 128))
            wpair_tgt[p] = [a, b, c]
            emit_wconv(p)

        load_pair_w(0)

        rz_src_free = [None, None]
        rz_dst_free = [None, None]
        onorm_free = [None, None]
        pt_free = [None, None, None]

        for p in range(NPAIR):
            sl = p % 2
            if p + 1 < NPAIR:
                load_pair_w(p + 1)
            for s, v in wpair_tgt[p]:
                wt("tensor", s, v)
            wt("vector", pe, pe.n)
            op("vector", lambda e: e.memset(vauga[:, :, 64:128], 1.0))
            v_vones = op("vector", lambda e: e.memset(vaugb[:, :, 0:64], 1.0), bump=ve)
            wt("tensor", ve, v_vones)

            last_evac = None
            for qc in range(NQC):
                b_ = qc % 2
                wait_bank(b_)
                def qmm(eng, sl=sl, qc=qc, b_=b_):
                    for c in range(8):
                        ins = eng.matmul(bank(b_), wqs[:, sl, c, :],
                                         xnT[:, c, qc * 512:(qc + 1) * 512],
                                         start=(c == 0), stop=(c == 7))
                    return ins
                v_mm = op("tensor", qmm, bump=pe)
                wt("vector", pe, v_mm)
                v_e = op("vector", lambda e, qc=qc, b_=b_: e.tensor_copy(
                    qT[:, qc * 512:(qc + 1) * 512], bank(b_)), bump=ve)
                set_guard(b_, ve, v_e)
            for qc in range(NQC):
                b_ = 2 + qc % 2
                wait_bank(b_)
                def kmm(eng, sl=sl, qc=qc, b_=b_):
                    for c in range(8):
                        ins = eng.matmul(bank(b_), wks[:, sl, c, :],
                                         xnT[:, c, qc * 512:(qc + 1) * 512],
                                         start=(c == 0), stop=(c == 7))
                    return ins
                v_mm = op("tensor", kmm, bump=pe)
                wt("vector", pe, v_mm)
                v_e = op("vector", lambda e, qc=qc, b_=b_: e.tensor_copy(
                    kT[:, qc * 512:(qc + 1) * 512], bank(b_)), bump=ve)
                set_guard(b_, ve, v_e)
            for g in range(8):
                b_ = 4 + g % 2
                wait_bank(b_)
                def vmm(eng, sl=sl, g=g, b_=b_):
                    for ttl in range(4):
                        tt = g * 4 + ttl
                        for c in range(8):
                            ins = eng.matmul(
                                bank(b_)[:, ttl * 128:(ttl + 1) * 128],
                                xnT[:, c, tt * 128:(tt + 1) * 128],
                                wvs[:, sl, c, :],
                                start=(c == 0), stop=(c == 7))
                    return ins
                v_mm = op("tensor", vmm, bump=pe)
                wt("vector", pe, v_mm)
                op("vector", lambda e, g=g, b_=b_: e.tensor_copy(
                    vauga[:, g * 4:(g + 1) * 4, 0:64],
                    bank(b_).rearrange("p (t j) -> p t j", t=4)[:, :, 0:64]))
                v_e = op("vector", lambda e, g=g, b_=b_: e.tensor_copy(
                    vaugb[:, g * 4:(g + 1) * 4, 64:128],
                    bank(b_).rearrange("p (t j) -> p t j", t=4)[:, :, 64:128]), bump=ve)
                set_guard(b_, ve, v_e)
                last_evac = v_e
            wt("tensor", ve, last_evac)
            qkv_done[sl] = pe.n

            for qc in range(NQC):
                oa = 6
                ob = 7
                wait_bank(oa)
                wait_bank(ob)
                pv_last = None
                exp_v = [None] * NKC
                pt_free3 = [None, None, None]

                def emit_s(kc, qc=qc):
                    sb0 = 2 * (kc % 3)
                    wait_bank(sb0)
                    wait_bank(sb0 + 1)
                    def smm(eng, qc=qc, kc=kc, sb0=sb0):
                        eng.matmul(bank(sb0), kT[0:64, kc * 128:(kc + 1) * 128],
                                   qT[0:64, qc * 512:(qc + 1) * 512],
                                   start=True, stop=True)
                        return eng.matmul(bank(sb0 + 1),
                                          kT[64:128, kc * 128:(kc + 1) * 128],
                                          qT[64:128, qc * 512:(qc + 1) * 512],
                                          start=True, stop=True)
                    v_s = op("tensor", smm, bump=pe)
                    pr = kc % 3
                    wt("scalar", pe, v_s)
                    if pt_free3[pr] is not None:
                        wt("scalar", pe, pt_free3[pr])
                    v_x = op("scalar", lambda e, pr=pr, sb0=sb0: e.activation(
                        pt[:, :, pr, :], bank(sb0, 2), AF.Exp,
                        bias=c_m20[:, :], scale=0.125), bump=ac)
                    set_guard(sb0, ac, v_x)
                    set_guard(sb0 + 1, ac, v_x)
                    exp_v[kc] = v_x

                def emit_pv(kc, qc=qc, oa=oa, ob=ob):
                    pr = kc % 3
                    wt("tensor", ac, exp_v[kc])
                    def pvmm(eng, kc=kc, pr=pr, oa=oa, ob=ob):
                        eng.matmul(bank(oa), vauga[:, kc, :], pt[:, 0, pr, :],
                                   start=(kc == 0), stop=(kc == NKC - 1))
                        return eng.matmul(bank(ob), vaugb[:, kc, :], pt[:, 1, pr, :],
                                          start=(kc == 0), stop=(kc == NKC - 1))
                    v_pv = op("tensor", pvmm, bump=pe)
                    pt_free3[pr] = v_pv
                    return v_pv

                emit_s(0)
                emit_s(1)
                for kc in range(2, NKC):
                    emit_s(kc)
                    pv_last = emit_pv(kc - 2)
                pv_last = emit_pv(NKC - 2)
                pv_last = emit_pv(NKC - 1)
                rs_ = qc % 2
                wt("vector", pe, pv_last)
                if onorm_free[rs_] is not None:
                    wt("vector", onorm_free[rs_][0], onorm_free[rs_][1])
                op("vector", lambda e, rs_=rs_: e.tensor_copy(
                    onorm[:, rs_, 0, :], bank(6)))
                v_oe = op("vector", lambda e, rs_=rs_: e.tensor_copy(
                    onorm[:, rs_, 1, :], bank(7)), bump=ve)
                set_guard(6, ve, v_oe)
                set_guard(7, ve, v_oe)
                wt("scalar", ve, v_oe)
                if rz_src_free[rs_] is not None:
                    wt("scalar", rz_src_free[rs_][0], rz_src_free[rs_][1])
                op("scalar", lambda e, rs_=rs_: e.activation(
                    rz_src[64:128, rs_, :], onorm[64:128, rs_, 0, :], AF.Ln))
                op("scalar", lambda e, rs_=rs_: e.activation(
                    rz_src[0:64, rs_, :], onorm[0:64, rs_, 1, :], AF.Ln))
                drain("scalar")
                op("scalar", lambda e, rs_=rs_: e.activation(
                    rz_src[64:128, rs_, :], rz_src[64:128, rs_, :], AF.Exp, scale=-1.0))
                v_rz = op("scalar", lambda e, rs_=rs_: e.activation(
                    rz_src[0:64, rs_, :], rz_src[0:64, rs_, :], AF.Exp, scale=-1.0), bump=ac)
                onorm_free[rs_] = (ac.sem, v_rz)
                mw = [(ac, v_rz)]
                if rz_dst_free[rs_] is not None:
                    mw.append(rz_dst_free[rs_])
                m1s, m1v = dma("scalar", mvq, rz_dst[0:64, rs_, :],
                               rz_src[64:128, rs_, :], waits=mw)
                m2s, m2v = dma("scalar", mvq, rz_dst[64:128, rs_, :],
                               rz_src[0:64, rs_, :])
                rz_src_free[rs_] = (m2s, m2v)
                wt("vector", m1s, m1v)
                wt("vector", m2s, m2v)
                op("vector", lambda e, p=p, qc=qc, rs_=rs_: e.tensor_mul(
                    bigB[0:64, p, qc * 512:(qc + 1) * 512],
                    onorm[0:64, rs_, 0, :], rz_dst[0:64, rs_, :]))
                v_n = op("vector", lambda e, p=p, qc=qc, rs_=rs_: e.tensor_mul(
                    bigB[64:128, p, qc * 512:(qc + 1) * 512],
                    onorm[64:128, rs_, 1, :], rz_dst[64:128, rs_, :]), bump=ve)
                rz_dst_free[rs_] = (ve, v_n)
        s1b.close()
        s1.close()

        probs_et = sb("probs_et", [8, T], F32)
        probs_tok = sb("probs_tok", [128, NT, NE], F32)
        slot_tok = sb("slot_tok", [128, NT, NE], U32)
        s2 = es.enter_context(ExitStack())
        w_outb = sb("w_outb", [128, 8, D], BF16, stack=s2)
        borep = sb("borep", [128, D], F32, stack=s2)
        bobuf = sb("bobuf", [1, D], F32, stack=s2)
        dma("sync", ldx, bobuf[:, :], b_out, waits=phase_waits())
        ld_bo = ldx.total_waits()
        r2r = sb("r2r", [128, 2, D], F32, stack=s2)
        x1br = sb("x1br", [128, 2, D], BF16, stack=s2)
        sexp = sb("sexp", [128, 2, NE], F32, stack=s2)
        xr2 = sb("xr2", [128, 3, D], F32, stack=s2)
        xr2_free = [None, None, None]
        srec = sb("srec", [128, 2, 1], F32, stack=s2)

        dma("gpsimd", ldw, w_outb[:, :, :],
            bass.AP(w_out, 0, [[D, 128], [D * 128, 8], [1, D]]),
            waits=phase_waits())
        w_wout = ldw.total_waits()

        wait_bank(0)
        wait_bank(1)
        wt("tensor", ve, ve.n)
        def borep_mms(eng):
            for s, v in ld_bo:
                eng.wait_ge(s, v)
            eng.matmul(bank(0), ones32[:, :], bobuf[:, 0:512], start=True, stop=True)
            return eng.matmul(bank(1), ones32[:, :], bobuf[:, 512:1024], start=True, stop=True)
        v_bo = op("tensor", borep_mms, bump=pe)
        wt("vector", pe, v_bo)
        v_bo2 = op("vector", lambda e: e.tensor_copy(borep[:, :], bank(0, 2)), bump=ve)
        set_guard(0, ve, v_bo2)
        set_guard(1, ve, v_bo2)

        first_wout = True
        x1br_free = [None, None]
        for t in range(NT):
            sl = t % 2
            slx = t % 3
            w = [] if xr2_free[slx] is None else [(ve, xr2_free[slx])]
            s_ld, v_ld = dma("sync", ldx, xr2[:, slx, :],
                             bass.AP(x_in, t * 128 * D, [[D, 128], [1, D]]),
                             waits=w)
            ob0 = 0 if sl == 0 else 2
            wait_bank(ob0)
            wait_bank(ob0 + 1)
            def outmm(eng, t=t, ob0=ob0, fw=first_wout):
                if fw:
                    for s, v in w_wout:
                        eng.wait_ge(s, v)
                for ic in range(8):
                    for dmc in range(2):
                        ins = eng.matmul(
                            bank(ob0 + dmc),
                            bigB[:, ic, t * 128:(t + 1) * 128],
                            w_outb[:, ic, dmc * 512:(dmc + 1) * 512],
                            start=(ic == 0), stop=(ic == 7))
                return ins
            v_om = op("tensor", outmm, bump=pe)
            first_wout = False
            wt("vector", pe, v_om)
            wt("vector", s_ld, v_ld)
            op("vector", lambda e, sl=sl, ob0=ob0: e.tensor_add(
                r2r[:, sl, :], bank(ob0, 2), borep[:, :]))
            drain("vector")
            v_r2 = op("vector", lambda e, sl=sl, slx=slx: e.tensor_add(
                r2r[:, sl, :], r2r[:, sl, :], xr2[:, slx, :]), bump=ve)
            xr2_free[slx] = v_r2
            set_guard(ob0, ve, v_r2)
            set_guard(ob0 + 1, ve, v_r2)
            drain("vector")
            op("vector", lambda e, sl=sl: e.bn_stats(bnr[:, sl, 0, :], r2r[:, sl, 0:512]))
            op("vector", lambda e, sl=sl: e.bn_stats(bnr[:, sl, 1, :], r2r[:, sl, 512:1024]))
            drain("vector")
            v_bn = op("vector", lambda e, sl=sl: e.bn_aggr(mvr[:, sl, :], bnr[:, sl, :, :]), bump=ve)
            wt("scalar", ve, v_bn)
            op("scalar", lambda e, sl=sl: e.activation(rsr[:, sl, :], mvr[:, sl, 1:2], AF.Ln, bias=c_eps[:, :]))
            drain("scalar")
            v_rs = op("scalar", lambda e, sl=sl: e.activation(rsr[:, sl, :], rsr[:, sl, :], AF.Exp, scale=-0.5), bump=ac)
            wt("vector", ac, v_rs)
            if x1br_free[sl] is not None:
                wt("vector", x1br_free[sl][0], x1br_free[sl][1])
            v_x1 = op("vector", lambda e, sl=sl: e.tensor_scalar(
                x1br[:, sl, :], r2r[:, sl, :], mvr[:, sl, 0:1], rsr[:, sl, 0:1],
                ALU.subtract, ALU.mult), bump=ve)
            x1br_free[sl] = dma("sync", stx, bass.AP(x1_scr, t * 128 * D, [[D, 128], [1, D]]),
                                x1br[:, sl, :], waits=[(ve, v_x1)])
            tb = 4 + sl
            wait_bank(tb)
            wt("tensor", ve, v_x1)
            def tr2(eng, sl=sl, tb=tb):
                for c in range(8):
                    ins = eng.matmul(bank16(tb)[:, c * 128:(c + 1) * 128],
                                     x1br[:, sl, c * 128:(c + 1) * 128],
                                     id16[:, :], start=True, stop=True,
                                     is_transpose=True)
                return ins
            v_tr = op("tensor", tr2, bump=pe)
            wt("vector", pe, v_tr)
            v_ev = op("vector", lambda e, t=t, tb=tb: e.tensor_copy(
                bigB[:, :, t * 128:(t + 1) * 128],
                bank16(tb).rearrange("p (c j) -> p c j", c=8)), bump=ve)
            set_guard(tb, ve, v_ev)
            x1T_ready = v_ev
            wait_bank(6)
            wt("tensor", ve, x1T_ready)
            def gatemm(eng, t=t, fw=(t == 0)):
                if fw:
                    for s, v in w_gate:
                        eng.wait_ge(s, v)
                for c in range(8):
                    ins = eng.matmul(bank(6)[:, 0:NE],
                                     bigB[:, c, t * 128:(t + 1) * 128],
                                     gate_wb[:, c, :],
                                     start=(c == 0), stop=(c == 7))
                return ins
            v_gm = op("tensor", gatemm, bump=pe)
            wt("scalar", pe, v_gm)
            v_sx = op("scalar", lambda e, sl=sl: e.activation(
                sexp[:, sl, :], bank(6)[:, 0:NE], AF.Exp), bump=ac)
            set_guard(6, ac, v_sx)
            wt("vector", ac, v_sx)
            op("vector", lambda e, sl=sl: e.reduce_sum(srec[:, sl, :], sexp[:, sl, :], AX.X))
            drain("vector")
            op("vector", lambda e, sl=sl: e.reciprocal(srec[:, sl, :], srec[:, sl, :]))
            drain("vector")
            v_pr = op("vector", lambda e, sl=sl, t=t: e.tensor_scalar(
                probs_tok[:, t, :], sexp[:, sl, :], srec[:, sl, 0:1], None,
                ALU.mult), bump=ve)
            wait_bank(7)
            wt("tensor", ve, v_pr)
            v_pt = op("tensor", lambda e, t=t: e.matmul(
                bank(7)[0:8, 0:128], probs_tok[:, t, :], id32[:, :],
                start=True, stop=True, is_transpose=True), bump=pe)
            wt("vector", pe, v_pt)
            v_pe2 = op("vector", lambda e, t=t: e.tensor_copy(
                probs_et[:, t * 128:(t + 1) * 128], bank(7)[0:8, 0:128]), bump=ve)
            set_guard(7, ve, v_pe2)
        probs_et_done = ve.n
        s2.close()

        s3 = es.enter_context(ExitStack())
        giq = DmaQ(nc, es, "giq", 8)
        tokidu = sb("tokidu", [128, NT], U32, stack=s3)
        s3a = ExitStack()
        probs_rep = sb("probs_rep", [128, T], F32, stack=s3a)
        maskb = sb("maskb", [128, T], BF16, stack=s3a)
        bis = sb("bis", [128, 8], F32, stack=s3a)
        posb = sb("posb", [8, T], F32, stack=s3a)
        dmpb = sb("dmpb", [8, T], F32, stack=s3a)
        scr8 = sb("scr8", [8, T], F32, stack=s3a)
        offrow = sb("offrow", [128, NE], F32, stack=s3a)

        for s_, c_ in phase_waits():
            wt("vector", s_.sem if isinstance(s_, Ctr) else s_, c_)
        for qc in range(NQC):
            b_ = qc % 4
            wait_bank(b_)
            def prep_mm(eng, qc=qc, b_=b_, vv=probs_et_done):
                if qc == 0:
                    eng.wait_ge(ve.sem, vv)
                return eng.matmul(bank(b_), selrep[:, :],
                                  probs_et[:, qc * 512:(qc + 1) * 512],
                                  start=True, stop=True)
            v_mm = op("tensor", prep_mm, bump=pe)
            wt("vector", pe, v_mm)
            v_e = op("vector", lambda e, qc=qc, b_=b_: e.tensor_copy(
                probs_rep[:, qc * 512:(qc + 1) * 512], bank(b_)), bump=ve)
            set_guard(b_, ve, v_e)

        drain("vector")
        op("vector", lambda e: e.memset(bis[:, 0:1], 0.0))
        op("vector", lambda e: e.memset(bis[:, 1:2], 1.0))
        for it in range(BIS_ITERS):
            drain("vector")
            op("vector", lambda e: e.tensor_scalar(
                bis[:, 2:3], bis[:, 0:1], bis[:, 1:2], 0.5, ALU.add, ALU.mult))
            drain("vector")
            op("vector", lambda e: e.tensor_scalar(
                maskb[:, :], probs_rep[:, :], bis[:, 2:3], 0.0,
                ALU.is_ge, ALU.add, accum_out=bis[:, 3:4]))
            drain("vector")
            op("vector", lambda e: e.tensor_scalar(
                bis[:, 4:5], bis[:, 3:4], 1024.0, None, ALU.is_ge))
            op("vector", lambda e: e.tensor_tensor(bis[:, 5:6], bis[:, 2:3], bis[:, 0:1], ALU.subtract))
            drain("vector")
            op("vector", lambda e: e.scalar_tensor_tensor(
                bis[:, 0:1], bis[:, 5:6], bis[:, 4:5], bis[:, 0:1], ALU.mult, ALU.add))
            op("vector", lambda e: e.tensor_tensor(bis[:, 6:7], bis[:, 1:2], bis[:, 2:3], ALU.subtract))
            drain("vector")
            op("vector", lambda e: e.scalar_tensor_tensor(
                bis[:, 1:2], bis[:, 6:7], bis[:, 4:5], bis[:, 2:3], ALU.mult, ALU.add))
        drain("vector")
        op("vector", lambda e: e.tensor_scalar(
            maskb[:, :], probs_rep[:, :], bis[:, 0:1], None, ALU.is_ge))
        drain("vector")
        drain("vector")
        op("vector", lambda e: e.tensor_tensor_scan(
            posb[:, :], maskb[0:8, :], maskb[0:8, :], 0.0, ALU.add, ALU.max))
        drain("vector")
        op("vector", lambda e: e.tensor_tensor(posb[:, :], posb[:, :], maskb[0:8, :], ALU.subtract))
        for s_, c_ in phase_waits():
            wt("gpsimd", s_.sem if isinstance(s_, Ctr) else s_, c_)
        op("gpsimd", lambda e: e.iota(dmpb[:, :], [[0, 32], [1, 128]], base=1024,
                                      channel_multiplier=0,
                                      allow_small_or_imprecise_dtypes=True))
        drain("gpsimd")
        gp_io = Ctr(nc, es, "gpio")
        on("gpsimd", lambda e: e.iota(tokidu[:, :], [[128, NT]], base=0,
                                      channel_multiplier=1,
                                      allow_small_or_imprecise_dtypes=True))
        drain("gpsimd")
        op("gpsimd", lambda e: e.iota(offrow[:, :], [[1152, NE]], base=0,
                                      channel_multiplier=0,
                                      allow_small_or_imprecise_dtypes=True), bump=gp_io)
        drain("vector")
        op("vector", lambda e: e.tensor_scalar(
            scr8[:, :], posb[:, :], 1023.0, None, ALU.is_le))
        drain("vector")
        op("vector", lambda e: e.tensor_tensor(scr8[:, :], scr8[:, :], maskb[0:8, :], ALU.mult))
        wt("vector", gp_io, gp_io.n)
        op("vector", lambda e: e.tensor_tensor(posb[:, :], posb[:, :], dmpb[:, :], ALU.subtract))
        drain("vector")
        op("vector", lambda e: e.tensor_tensor(posb[:, :], posb[:, :], scr8[:, :], ALU.mult))
        drain("vector")
        v_slot = op("vector", lambda e: e.tensor_tensor(posb[:, :], posb[:, :], dmpb[:, :], ALU.add), bump=ve)
        wt("tensor", ve, v_slot)
        for t in range(NT):
            b_ = t % 2
            wait_bank(b_)
            v_tr = op("tensor", lambda e, t=t, b_=b_: e.matmul(
                bank(b_)[:, 0:8], posb[0:8, t * 128:(t + 1) * 128], id32[0:8, 0:8],
                start=True, stop=True, is_transpose=True), bump=pe)
            wt("vector", pe, v_tr)
            op("vector", lambda e, t=t, b_=b_: e.tensor_tensor(
                bank(b_)[:, 0:8], bank(b_)[:, 0:8], offrow[:, :], ALU.add))
            drain("vector")
            v_ev = op("vector", lambda e, t=t, b_=b_: e.tensor_copy(
                slot_tok[:, t, :], bank(b_)[:, 0:8]), bump=ve)
            set_guard(b_, ve, v_ev)
        slot_done = ve.n
        meta_st = dma("gpsimd", acq,
                      bass.AP(meta_scr, 0, [[NE, 128], [NE * 128, NT], [1, NE]]),
                      probs_tok[:, :, :], waits=[(ve, ve.n)])
        s3a.close()

        spill_done = stx.total_waits() + [meta_st]
        toksl_ready = {}

        def emit_tokscat(e_):
            tgts = {}
            for t in range(NT):
                w = []
                if e_ == 0 and t == 0:
                    w.append((ve, slot_done))
                semh, tgt, pre = giq.issue()
                def fts(eng, t=t, e_=e_, s=semh, pre=pre, w=mkwaits(w)):
                    if pre is not None:
                        eng.wait_ge(pre[0], pre[1])
                    for ws, wv in w:
                        eng.wait_ge(ws, wv)
                    eng.indirect_dma_start(
                        bass.AP(toksl_scr, 0, [[1, NE * 1152], [1, 1]]),
                        IndirectOffsetOnAxis(slot_tok[:, t, e_:e_ + 1], 0),
                        tokidu[:, t:t + 1], None,
                        bounds_check=bc_regs[e_], oob_is_err=False,
                    ).then_inc(s, 16)
                on("gpsimd", fts)
                tgts[semh] = tgt
            toksl_ready[e_] = list(tgts.items())

        x1l = sb("x1l", [128, 2, D], BF16, stack=s3)
        r3r = sb("r3r", [128, 2, D], F32, stack=s3)
        outr = sb("outr", [128, 1, D], F32, stack=s3)
        x1l_free = [None, None]
        outr_free = [None, None]
        s3b = es.enter_context(ExitStack())
        w1full = sb("w1full", [128, 8, HID], BF16, stack=s3b)
        w2s = sb("w2s", [128, 2, 4, D], BF16, stack=s3b)
        yscat = sb("yscat", [128, 2, D], F32, stack=s3b)
        probg = sb("probg", [128, 2, 8, NE], BF16, stack=s3b)
        probf = sb("probf", [128, 2, 8, NE], F32, stack=s3b)
        tokiu = sb("tokiu", [128, 2, 8, 1], U32, stack=s3b)
        w2_free = [None, None]
        yscat_free = [None, None]
        mload_free = [None, None]
        tokiu_free = [None, None]
        ysc = bigB[:, 0, :].rearrange("p (c j) -> p c j", c=8)

        def hT_v(hc):
            return bigB[:, hc // 8, (hc % 8) * 512:(hc % 8) * 512 + 512]

        def xetok_v(r):
            return bigB[:, 4 + r, :]

        def xeT_v(r, cdm):
            return bigB[:, 6 + r, cdm * 512:(cdm + 1) * 512]

        def w1_ap(e_):
            return bass.AP(w1bf, e_ * D * HID, [[HID, 128], [HID * 128, 8], [1, HID]])

        def w2_ap(e_, hc0):
            return bass.AP(w2bf, e_ * HID * D + hc0 * 128 * D,
                           [[D, 128], [128 * D, 4], [1, D]])

        w1_tgt = dma("sync", wfq, w1full[:, :, :], w1_ap(0),
                     waits=phase_waits() + [wconv[0][0]])
        xetok_tgt = {}
        tok_ring_free = [None, None]
        xeT_ring_free = [None, None]
        first_scat = [True]
        first_pg = [True]
        cur_scats = [[]]
        scats_all = {}

        def emit_xetok_loads(e_):
            r2 = e_ % 2
            wti = list(toksl_ready[e_])
            if e_ - 2 >= 0:
                wti += scats_all[e_ - 2]
            tld = []
            for i in range(8):
                tld.append(dma("sync", ldx, tokiu[:, r2, i, :],
                               bass.AP(toksl_scr, e_ * 1152 + i * 128,
                                       [[1, 128], [1, 1]]),
                               waits=(wti if i == 0 else [])))
            for blk in range(2):
                r = blk % 2
                gl = []
                for g in range(4):
                    i = blk * 4 + g
                    gw = [tld[i]]
                    if e_ == 0 and i == 0:
                        gw += list(spill_done)
                    if g == 0 and tok_ring_free[r] is not None:
                        gw.append((pe.sem, tok_ring_free[r]))
                    semh, tgt, pre = giq.issue()
                    def fxg(eng, g=g, r=r, r2=r2, i=i, s=semh, pre=pre,
                            gw=mkwaits(gw)):
                        if pre is not None:
                            eng.wait_ge(pre[0], pre[1])
                        for ws, wv in gw:
                            eng.wait_ge(ws, wv)
                        eng.indirect_dma_start(
                            bass.AP(bigB, (4 + r) * T + g * 1024,
                                    [[8 * T, 128], [1, D]]),
                            None,
                            bass.AP(x1_scr, 0, [[D, T], [1, D]]),
                            IndirectOffsetOnAxis(tokiu[:, r2, i, 0:1], 0),
                        ).then_inc(s, 16)
                    on("gpsimd", fxg)
                    gl.append((semh, tgt))
                xetok_tgt[(e_, r)] = gl
            gsems = []
            for i in range(8):
                semh, tgt, pre = giq.issue()
                gw = [tld[i]]
                if first_pg[0]:
                    first_pg[0] = False
                    gw.append(meta_st)
                if mload_free[r2] is not None:
                    gw.append((mload_free[r2][0].sem, mload_free[r2][1]))
                def fpg(eng, i=i, r2=r2, s=semh, pre=pre, gw=mkwaits(gw)):
                    if pre is not None:
                        eng.wait_ge(pre[0], pre[1])
                    for ws, wv in gw:
                        eng.wait_ge(ws, wv)
                    eng.indirect_dma_start(
                        probg[:, r2, i, :], None,
                        bass.AP(meta_scr, 0, [[NE, T], [1, NE]]),
                        IndirectOffsetOnAxis(tokiu[:, r2, i, 0:1], 0),
                    ).then_inc(s, 16)
                on("gpsimd", fpg)
                gsems.append((semh, tgt))
            for s_, v_ in gsems:
                wt("vector", s_, v_)
            drain("vector")
            op("vector", lambda e, r2=r2: e.tensor_copy(
                probf[:, r2, :, :], probg[:, r2, :, :]), bump=ve)
            drain("vector")

        emit_tokscat(0)
        emit_xetok_loads(0)
        emit_tokscat(1)

        for e_ in range(NE):
            if e_ + 2 < NE:
                emit_tokscat(e_ + 2)
            for blk in range(2):
                r = blk % 2
                for sgh, sgt in xetok_tgt[(e_, r)]:
                    wt("tensor", sgh, sgt)
                for g in range(4):
                    tb_ = 4 + (g % 2)
                    wait_bank(tb_)
                    def xtr(eng, r=r, g=g, tb_=tb_):
                        for cdm in range(8):
                            ins = eng.matmul(
                                bank16(tb_)[:, cdm * 128:(cdm + 1) * 128],
                                xetok_v(r)[:, g * 1024 + cdm * 128:g * 1024 + (cdm + 1) * 128],
                                id16[:, :], start=True, stop=True,
                                is_transpose=True)
                        return ins
                    v_xt = op("tensor", xtr, bump=pe)
                    wt("vector", pe, v_xt)
                    if g == 0 and xeT_ring_free[r] is not None:
                        wt("vector", pe, xeT_ring_free[r])
                    def xev(eng, r=r, g=g, tb_=tb_):
                        return eng.tensor_copy(
                            bass.AP(bigB, (6 + r) * T + g * 128,
                                    [[8 * T, 128], [512, 8], [1, 128]]),
                            bank16(tb_).rearrange("p (c j) -> p c j", c=8))
                    v_xe = op("vector", xev, bump=ve)
                    set_guard(tb_, ve, v_xe)
                xeT_done = ve.n
                tok_ring_free[r] = pe.n
                if blk == 1 and e_ + 1 < NE:
                    emit_xetok_loads(e_ + 1)
                for hc in range(NHC):
                    b_ = hc % 2
                    wait_bank(b_)
                    def hmm(eng, hc=hc, r=r, b_=b_,
                            w1w=(w1_tgt if (blk == 0 and hc == 0) else None),
                            xw=(xeT_done if hc == 0 else None)):
                        if w1w is not None:
                            eng.wait_ge(w1w[0], w1w[1])
                        if xw is not None:
                            eng.wait_ge(ve.sem, xw)
                        for c in range(8):
                            ins = eng.matmul(
                                bank(b_),
                                w1full[:, c, hc * 128:(hc + 1) * 128],
                                xeT_v(r, c),
                                start=(c == 0), stop=(c == 7))
                        return ins
                    v_h = op("tensor", hmm, bump=pe)
                    wt("scalar", pe, v_h)
                    v_g = op("scalar", lambda e, hc=hc, b_=b_: e.activation(
                        hT_v(hc), bank(b_), AF.Gelu), bump=ac)
                    set_guard(b_, ac, v_g)
                xeT_ring_free[r] = pe.n
                gelu_all = ac.n
                for i in range(8):
                    wait_bank(i)
                wt("tensor", ac, gelu_all)
                v_y = None
                for hcg in range(8):
                    wsl = hcg % 2
                    w2w = []
                    if w2_free[wsl] is not None:
                        w2w.append((pe, w2_free[wsl]))
                    if hcg == 0:
                        w2w.append(wconv[e_][1])
                    w2_tgt = dma("sync", wfq if hcg % 2 == 0 else wfq2,
                                 w2s[:, wsl, :, :],
                                 w2_ap(e_, hcg * 4), waits=w2w)
                    for hci in range(4):
                        hc = hcg * 4 + hci
                        def ymm(eng, hc=hc, wsl=wsl, hci=hci,
                                w2w_=(w2_tgt if hci == 0 else None)):
                            if w2w_ is not None:
                                eng.wait_ge(w2w_[0], w2w_[1])
                            for cc in range(8):
                                ins = eng.matmul(
                                    bank(cc),
                                    w2s[:, wsl, hci, cc * 128:(cc + 1) * 128],
                                    hT_v(hc),
                                    start=(hc == 0), stop=(hc == NHC - 1))
                            return ins
                        v_y = op("tensor", ymm, bump=pe)
                    w2_free[wsl] = pe.n
                if blk == 1 and e_ + 1 < NE:
                    w1_tgt = dma("sync", wfq, w1full[:, :, :], w1_ap(e_ + 1),
                                 waits=[(pe, pe.n), wconv[e_ + 1][0]])
                wt("vector", pe, v_y)
                for cc in range(8):
                    v_ysc = op("vector", lambda e, cc=cc: e.tensor_copy(
                        ysc[:, cc, :], bank(cc)), bump=ve)
                    set_guard(cc, ve, v_ysc)
                ysc_done = ve.n
                for stt in range(4):
                    i = blk * 4 + stt
                    tb_ = stt % 2
                    sl2 = i % 2
                    r2 = e_ % 2
                    wait_bank(tb_)
                    wt("tensor", ve, ysc_done)
                    def ytr(eng, stt=stt, tb_=tb_):
                        for cc in range(8):
                            ins = eng.matmul(
                                bank16(tb_)[:, cc * 128:(cc + 1) * 128],
                                ysc[:, cc, stt * 128:(stt + 1) * 128],
                                id16[:, :], start=True, stop=True,
                                is_transpose=True)
                        return ins
                    v_t = op("tensor", ytr, bump=pe)
                    wt("vector", pe, v_t)
                    if yscat_free[sl2] is not None:
                        wt("vector", yscat_free[sl2][0], yscat_free[sl2][1])
                    drain("vector")
                    v_yt = op("vector", lambda e, i=i, sl2=sl2, r2=r2, tb_=tb_,
                              e_=e_: e.tensor_scalar(
                                  yscat[:, sl2, :], bank16(tb_),
                                  probf[:, r2, i, e_:e_ + 1], None, ALU.mult), bump=ve)
                    set_guard(tb_, ve, v_yt)
                    semh, tgt, pre = ysq.issue()
                    sw = [(ve.sem, v_yt)]
                    if i == 0:
                        sw += list(mkwaits(cur_scats[0]))
                        cur_scats[0] = []
                    if first_scat[0]:
                        first_scat[0] = False
                        sw += list(mkwaits(r3z_waits))
                    def fsc(eng, i=i, sl2=sl2, r2=r2, s=semh, pre=pre,
                            sw=tuple(sw)):
                        if pre is not None:
                            eng.wait_ge(pre[0], pre[1])
                        for ws, wv in sw:
                            eng.wait_ge(ws, wv)
                        eng.indirect_dma_start(
                            bass.AP(r3_scr, 0, [[D, T], [1, D]]),
                            IndirectOffsetOnAxis(tokiu[:, r2, i, 0:1], 0),
                            yscat[:, sl2, :], None,
                            compute_op=ALU.add,
                        ).then_inc(s, 16)
                    on("gpsimd", fsc)
                    yscat_free[sl2] = (semh, tgt)
                    cur_scats[0].append((semh, tgt))
                    scats_all.setdefault(e_, []).append((semh, tgt))
                    if blk == 1 and stt == 3:
                        mload_free[r2] = (ve, v_yt)
        s3b.close()

        r3_parts = stx.total_waits()
        scat_done = ysq.total_waits()
        r3f = [None, None]
        x1lf = [None, None]
        for t in range(NT):
            sl = t % 2
            w2_ = []
            if t == 0:
                w2_ += list(r3_parts) + list(scat_done)
            if r3f[sl] is not None:
                w2_.append(r3f[sl])
            s_b, v_b = dma("sync", ldx if t % 2 == 0 else wfq2, r3r[:, sl, :],
                           bass.AP(r3_scr, t * 128 * D, [[D, 128], [1, D]]),
                           waits=w2_)
            wx_ = []
            if t == 0:
                wx_ += list(r3_parts)
            if x1lf[sl] is not None:
                wx_.append(x1lf[sl])
            s_x, v_x = dma("scalar", mvq, x1l[:, sl, :],
                           bass.AP(x1_scr, t * 128 * D, [[D, 128], [1, D]]),
                           waits=wx_)
            wt("vector", s_b, v_b)
            wt("vector", s_x, v_x)
            drain("vector")
            v_add = op("vector", lambda e, sl=sl: e.tensor_tensor(
                r3r[:, sl, :], r3r[:, sl, :], x1l[:, sl, :], ALU.add), bump=ve)
            x1lf[sl] = (ve, v_add)
            drain("vector")
            op("vector", lambda e, sl=sl: e.bn_stats(bnr[:, sl, 0, :], r3r[:, sl, 0:512]))
            op("vector", lambda e, sl=sl: e.bn_stats(bnr[:, sl, 1, :], r3r[:, sl, 512:1024]))
            drain("vector")
            v_bn = op("vector", lambda e, sl=sl: e.bn_aggr(mvr[:, sl, :], bnr[:, sl, :, :]), bump=ve)
            wt("scalar", ve, v_bn)
            op("scalar", lambda e, sl=sl: e.activation(rsr[:, sl, :], mvr[:, sl, 1:2], AF.Ln, bias=c_eps[:, :]))
            drain("scalar")
            v_rs = op("scalar", lambda e, sl=sl: e.activation(rsr[:, sl, :], rsr[:, sl, :], AF.Exp, scale=-0.5), bump=ac)
            wt("vector", ac, v_rs)
            if outr_free[0] is not None:
                wt("vector", outr_free[0][0], outr_free[0][1])
            v_o = op("vector", lambda e, sl=sl: e.tensor_scalar(
                outr[:, 0, :], r3r[:, sl, :], mvr[:, sl, 0:1], rsr[:, sl, 0:1],
                ALU.subtract, ALU.mult), bump=ve)
            r3f[sl] = (ve, v_o)
            so, vo = dma("sync", stx, bass.AP(out, t * 128 * D, [[D, 128], [1, D]]),
                         outr[:, 0, :], waits=[(ve, v_o)])
            outr_free[0] = (so, vo)
        s3.close()

        for s, c in stx.total_waits():
            wt("sync", s, c)

        with nc.Block() as block:
            @block.sync
            def _(eng):
                for f in plan["sync"]:
                    f(eng)

            @block.gpsimd
            def _(eng):
                for f in plan["gpsimd"]:
                    f(eng)

            @block.vector
            def _(eng):
                for f in plan["vector"]:
                    f(eng)

            @block.scalar
            def _(eng):
                for f in plan["scalar"]:
                    f(eng)

            @block.tensor
            def _(eng):
                for f in plan["tensor"]:
                    f(eng)

    return nc


_NC_CACHE = None


def kernel(L0_attn_x, L0_attn_ln_g, L0_attn_ln_b, L0_attn_w_qkv, L0_attn_w_out,
           L0_attn_b_out, L0_ffn_gate_w, L0_ffn_w1, L0_ffn_w2):
    global _NC_CACHE
    x = np.ascontiguousarray(np.asarray(L0_attn_x, dtype=np.float32))
    ln_g = np.ascontiguousarray(np.asarray(L0_attn_ln_g, dtype=np.float32)).reshape(1, D)
    ln_b = np.ascontiguousarray(np.asarray(L0_attn_ln_b, dtype=np.float32)).reshape(1, D)
    wqkv = np.ascontiguousarray(np.asarray(L0_attn_w_qkv, dtype=np.float32))
    wout = np.ascontiguousarray(np.asarray(L0_attn_w_out, dtype=np.float32))
    bout = np.ascontiguousarray(np.asarray(L0_attn_b_out, dtype=np.float32)).reshape(1, D)
    gw = np.ascontiguousarray(np.asarray(L0_ffn_gate_w, dtype=np.float32))
    w1a = np.ascontiguousarray(np.asarray(L0_ffn_w1, dtype=np.float32))
    w2a = np.ascontiguousarray(np.asarray(L0_ffn_w2, dtype=np.float32))

    if _NC_CACHE is None:
        _NC_CACHE = build()
    nc = _NC_CACHE

    in_maps = []
    for i in range(8):
        in_maps.append({
            "x": x[i], "ln_g": ln_g, "ln_b": ln_b, "w_qkv": wqkv,
            "w_out": wout, "b_out": bout, "gate_w": gw, "w1": w1a, "w2": w2a,
        })
    res = run_bass_kernel_spmd(nc, in_maps, core_ids=list(range(8)))
    return np.stack([r["out"] for r in res.results], axis=0)
